# version 06
# baseline (speedup 1.0000x reference)
import numpy as np
import ml_dtypes
import concourse.bass as bass
import concourse.mybir as mybir
from concourse.bass import IndirectOffsetOnAxis
from concourse.bass_utils import run_bass_kernel_spmd
from contextlib import ExitStack

F32 = mybir.dt.float32
BF16 = mybir.dt.bfloat16
I32 = mybir.dt.int32
U32 = mybir.dt.uint32
AF = mybir.ActivationFunctionType
ALU = mybir.AluOpType
AX = mybir.AxisListType

NCORE = 8
B, S, D = 2, 8192, 1024
TOK = B * S
TLOC = TOK // NCORE
NTILE_G = TOK // 128
NTILE_L = TLOC // 128
NTB = NTILE_G // 2
CHUNK = 64
NC = 2 * NTB
INTER = 2048
FF = 4096
NEXP = 16
NLAY = 4
CAP = 256
EBROWS = NEXP * CAP + 8
DUMP = NEXP * CAP
EPS = 1e-6
LN_EPS = 1e-5
KAPPA = 2.0 / (128 * 1024)
BASE_LR = 0.01

ENGS = ("tensor", "vector", "scalar", "gpsimd", "sync")
STOP_AFTER = None
SCANPART = 99
GTPART = 9
FFNPART = 99


PSUM_BUFS = ("psA", "psB", "psC")


class _Buf:
    __slots__ = ("writer", "readers", "last_read")

    def __init__(self):
        self.writer = None
        self.readers = []
        self.last_read = {}


class Sched:

    def __init__(self):
        self.ops = {e: [] for e in ENGS}
        self.cum = {}
        self.mark = {}
        self.bufs = {}
        self.pending_fence = None
        self.fenced = set()
        self.final_waits = []
        self.sem_keys = []

    def _key(self, k):
        if k not in self.cum:
            self.cum[k] = 0
            self.sem_keys.append(k)
        return k

    def buf(self, name):
        b = self.bufs.get(name)
        if b is None:
            b = self.bufs[name] = _Buf()
        return b

    def fence(self):
        self.pending_fence = dict(self.cum)
        self.fenced = set()

    def op(self, eng, fn, reads=(), writes=(), inc=1, waits=(), stream=None,
           no_waw=False):
        key = self._key(stream if stream is not None else eng)
        need = {}

        def add(pe, pk, pv):
            if pe == eng and eng == "tensor" and pk == "tensor":
                return
            if pk not in ENGS:
                pv = self.cum[pk]
            need[pk] = max(need.get(pk, 0), pv)

        if inc == 16:
            prev = self.cum[key]
            if prev > 0:
                need[key] = max(need.get(key, 0), prev)
        if self.pending_fence is not None and eng not in self.fenced:
            for pk, pv in self.pending_fence.items():
                if pv > 0 and not (pk == eng == "tensor"):
                    need[pk] = max(need.get(pk, 0), pv)
            self.fenced.add(eng)
        for nm in reads:
            b = self.buf(nm)
            if b.writer:
                add(*b.writer)
            if nm.startswith("ps") and eng in ("scalar", "vector"):
                other = "vector" if eng == "scalar" else "scalar"
                lr = b.last_read.get(other)
                if lr:
                    add(*lr)
        for nm in writes:
            b = self.buf(nm)
            if b.writer and not no_waw:
                add(*b.writer)
            for r in b.readers:
                add(*r)
        for pk, pv in waits:
            need[pk] = max(need.get(pk, 0), pv)
        final = {}
        for pk, pv in need.items():
            if pk == key and inc == 1 and pv >= self.cum[key] + inc:
                continue
            if self.mark.get((eng, pk), 0) >= pv:
                continue
            final[pk] = pv
            self.mark[(eng, pk)] = pv
        self.cum[key] += inc
        mycum = self.cum[key]
        self.ops[eng].append((final, fn, key, inc))
        for nm in reads:
            b = self.buf(nm)
            b.readers.append((eng, key, mycum))
            b.last_read[eng] = (eng, key, mycum)
        for nm in writes:
            b = self.buf(nm)
            b.writer = (eng, key, mycum)
            b.readers = []
            b.last_read = {}
        return (key, mycum)

    def barrier_of(self, names):
        w = {}
        for nm in names:
            b = self.bufs.get(nm)
            if b is not None and b.writer:
                pe, pk, pv = b.writer
                if pk not in ENGS:
                    pv = self.cum[pk]
                w[pk] = max(w.get(pk, 0), pv)
        return list(w.items())


def build_program(taps=()):
    nc = bass.Bass("TRN2", target_bir_lowering=False, debug=False)
    taps = set(taps)

    def din(name, shape, dtype=F32):
        return nc.dram_tensor(name, list(shape), dtype, kind="ExternalInput").ap()

    x_full = din("x_full", [TOK, D])
    x_loc = din("x_loc", [TLOC, D])
    wkT = din("wkT", [D, D])
    wvT = din("wvT", [D, D])
    wqT = din("wqT", [D, D])
    knorm = din("knorm", [1, D])
    qnorm = din("qnorm", [1, D])
    threp_d = din("threp", [128, D])
    alrep_d = din("alrep", [128, D])
    etrep_d = din("etrep", [128, D])
    w0T_d = din("w0T", [D, INTER])
    w1n_d = din("w1n", [D, INTER])
    lnw_d = din("lnw", [1, D])
    routers_d = din("routers", [NLAY, D, NEXP])
    W1f_d = din("W1f", [NEXP, D, FF])
    W2f_d = din("W2f", [NEXP, FF, D])
    identb_d = din("identb", [128, 128], BF16)
    identf_d = din("identf", [128, 128], F32)
    trib_d = din("trib", [128, 128], BF16)
    iota16_d = din("iota16", [1, 16])
    ones128b_d = din("ones128b", [128, 1], BF16)
    onesK1f_d = din("onesK1f", [1, 128])
    halfones_d = din("halfones", [128, 2])
    zeb_d = din("zeb", [EBROWS, 1032], BF16)
    zebo_d = din("zebo", [EBROWS, D])

    out_x2 = nc.dram_tensor("out_x2", [TLOC, D], F32, kind="ExternalOutput").ap()

    def scr(name, shape, dtype):
        return nc.dram_tensor(name, list(shape), dtype, kind="Internal").ap()

    ktok_d = scr("ktok_d", [NC, 128, D], BF16)
    kmv_d = scr("kmv_d", [NC, 128, D], BF16)
    qtok_d = scr("qtok_d", [TLOC, D], BF16)
    zd_d = scr("zd_d", [2, 3 * NTILE_G], F32)
    th_d = scr("th_d", [2, NC], F32)
    etab_d = scr("etab_d", [2, NC], F32)
    w1n_sc = scr("w1n_sc", [D, INTER], BF16)
    ln_sc = scr("ln_sc", [1, D], F32)
    W1b_d = scr("W1b_d", [NEXP, D, FF], BF16)
    W2b_d = scr("W2b_d", [NEXP, FF, D], BF16)
    x1_sc = scr("x1_sc", [TLOC, D], F32)
    eb_d = scr("eb_d", [EBROWS, 1032], BF16)
    ebo_d = scr("ebo_d", [EBROWS, D], F32)

    tap_out = {}

    def tapt(name, shape, dtype=F32):
        if name in taps and name not in tap_out:
            tap_out[name] = nc.dram_tensor("tap_" + name, list(shape), dtype,
                                           kind="ExternalOutput").ap()
        return tap_out.get(name)

    sch = Sched()
    stack = ExitStack()

    def sb(name, shape, dtype):
        return stack.enter_context(nc.sbuf_tensor(name, list(shape), dtype))

    W0T = sb("W0T", [128, 8 * INTER], BF16)
    W1N = sb("W1N", [128, 8 * INTER], BF16)
    S0 = sb("S0", [128, 8 * INTER], BF16)
    S1 = sb("S1", [128, 8 * INTER], BF16)
    W1TW = sb("W1TW", [128, 8 * 1024], BF16)
    BIGF = sb("BIGF", [128, 4096], F32)
    BIGFb = BIGF[:].bitcast(BF16)
    GSB = sb("GSB", [128, 4096], BF16)
    GSBf = GSB[:].bitcast(F32)
    TRASH = sb("TRASH", [128, INTER], BF16)
    TRASHF = TRASH[:].bitcast(F32)
    HTOK = sb("HTOK", [128, INTER], BF16)
    HTOKf = HTOK[:].bitcast(F32)
    HT = sb("HT", [128, INTER], BF16)
    T0 = sb("T0", [128, D], BF16)
    E0 = sb("E0", [128, D], BF16)
    T1 = sb("T1", [128, D], BF16)
    DA = sb("DA", [128, INTER], BF16)
    DY = sb("DY", [128, D], BF16)
    ZLN = T0
    DYT = E0
    KTOK = sb("KTOK", [128, D], BF16)
    KMV = sb("KMV", [128, D], BF16)
    LNMF = sb("LNMF", [128, D], F32)
    SLNR = sb("SLNR", [128, D], BF16)
    PAN = sb("PAN", [128, 3 * NC + 64], F32)
    THCOL = PAN[:, 0:NC]
    ETACOL = PAN[:, NC:2 * NC]
    BETACOL = PAN[:, 2 * NC:3 * NC]
    NORMP = PAN[:, 3 * NC:3 * NC + 34]
    ZSB = PAN[0:2, 0:3 * NTILE_G]
    THsb = PAN[32:34, 0:NC]
    ALsb = PAN[32:34, NC:2 * NC]
    ETsb = PAN[32:34, 2 * NC:3 * NC]
    TH2 = PAN[64:66, 0:NC]
    EB2 = PAN[64:65, NC:3 * NC]
    SP = sb("SP", [128, 8], F32)
    SR = sb("SR", [1, 8], F32)
    RC = sb("RC", [128, 8], F32)
    MX8 = sb("MX8", [128, 8], F32)
    IXU = sb("IXU", [128, 8], U32)
    SELG = sb("SELG", [128, 2], F32)
    EXW = sb("EXW", [128, 16], F32)
    IDXF = sb("IDXF", [128, 1], F32)
    ISB = sb("ISB", [128, 16], BF16)
    ISF = sb("ISF", [128, 16], F32)
    POSW = sb("POSW", [128, 16], F32)
    RCNT = sb("RCNT", [1, 16], F32)
    SLOTF = sb("SLOTF", [128, 1], F32)
    SLOTI = sb("SLOTI", [128, 2], I32)
    SLOT_ALL = sb("SLOT_ALL", [128, 16], I32)
    ROUT = sb("ROUT", [128, 8 * 16], F32)
    IDENTB = sb("IDENTB", [128, 128], BF16)
    IDENTF = sb("IDENTF", [128, 128], F32)
    TRIB = sb("TRIB", [128, 128], BF16)
    IOTA = sb("IOTA", [128, 16], F32)
    ONESB = sb("ONESB", [128, 1], BF16)
    ONESK1 = sb("ONESK1", [1, 128], F32)
    HALFON = sb("HALFON", [128, 2], F32)
    ONESF = sb("ONESF", [128, 1], F32)
    ONESH = sb("ONESH", [128, 1], F32)
    SG = BIGF[:, 0:2048]
    UB = BIGF[:, 2048:4096]
    KT = TRASH[:, 0:1024]
    XT_A = BIGF[:, 0:2048]
    KDST_A = GSB[:, 0:1024]
    XTT_A = HT[:, 0:1024]
    KP_A = BIGF[:, 2048:3072]
    VT_A = DA[:, 0:1024]
    QT_A = DA[:, 1024:2048]
    SGA = HTOKf
    T1A = BIGF[:, 3072:4096]
    WKT = S1[:, 0:8 * 1024]
    WVT = W1TW[:]
    WQT = S1[:, 8 * 1024:16 * 1024]
    S0f = S0[:].bitcast(F32)
    THREP = S0f[:, 0:1024]
    ALREP = S0f[:, 1024:2048]
    ETREP = S0f[:, 2048:3072]
    KNREP = S0f[:, 3072:4096]
    QNREP = S0f[:, 4096:5120]
    ZPAN = S0f[:, 5120:5120 + 3 * NTILE_G]
    QTOK_C = KTOK[:]
    XL_C = BIGF[:, 0:1024]
    ATT_C = BIGF[:, 2048:3072]
    X1_CC = BIGF[:, 3072:4096]
    XTF = BIGF[:, 0:1024]
    GBv = BIGFb[:, 2048:3080]
    XTGv = BIGFb[:, 4112:6160]
    SCT = HT[:, 0:1032]
    HFv = S0[:, 0:8192]
    OGv = S0f[:, 4096:6144]
    X1T_F = S0f[:, 6144:7168]
    X1_CF = S0f[:, 7168:8192]

    psA = stack.enter_context(nc.psum_tensor("psA", [128, 2048], F32))
    psB = stack.enter_context(nc.psum_tensor("psB", [128, 1024], F32))
    psC = stack.enter_context(nc.psum_tensor("psC", [128, 2048], BF16))
    psAb = psA[:].bitcast(BF16)
    psCf = psC[:].bitcast(F32)

    def mm(out_ap, lhsT_ap, rhs_ap, start, stop, rd, wr, waits=()):
        return sch.op("tensor",
                      lambda t, o=out_ap, l=lhsT_ap, r=rhs_ap, s0=start, s1=stop:
                      t.matmul(o, l, r, start=s0, stop=s1),
                      reads=rd, writes=wr, waits=waits)

    def transp(out_ap, in_ap, ident, identn, rd, wr, waits=()):
        return sch.op("tensor",
                      lambda t, o=out_ap, i=in_ap, e=ident: t.transpose(o, i, e),
                      reads=list(rd) + [identn], writes=wr, waits=waits)

    def act(out_ap, in_ap, func, rd, wr, scale=1.0, bias=0.0, accum=None, waits=()):
        return sch.op("scalar",
                      lambda s, o=out_ap, i=in_ap, f=func, sc=scale, b=bias, a=accum:
                      s.activation(out=o, in_=i, func=f, bias=b, scale=sc,
                                   accum_out=a),
                      reads=rd, writes=wr, waits=waits)

    def vop(fn, rd, wr, eng="vector", waits=()):
        return sch.op(eng, fn, reads=rd, writes=wr, waits=waits)

    def dma(eng, out_ap, in_ap, rd, wr, stream="ld0", waits=(), no_waw=False, **kw):
        return sch.op(eng,
                      lambda e, o=out_ap, i=in_ap, k=kw: e.dma_start(out=o, in_=i, **k),
                      reads=rd, writes=wr, inc=16, waits=waits, stream=stream,
                      no_waw=no_waw)

    def _l(x):
        return [x] if isinstance(x, str) else list(x)

    def sigtriple(dst_ap, dstn, src_ap, srcn, scale_neg, t1_ap, t1n, t2_ap, t2n,
                  extra_rd=()):
        act(t1_ap, src_ap, AF.Exp, _l(srcn) + list(extra_rd), _l(t1n),
            scale=scale_neg)
        vop(lambda v, t=t1_ap: v.tensor_scalar(out=t, in0=t, scalar1=1e18,
                                               scalar2=None, op0=ALU.min),
            _l(t1n), _l(t1n))
        act(t2_ap, t1_ap, AF.Ln, _l(t1n), _l(t2n), bias=1.0)
        act(dst_ap, t2_ap, AF.Exp, _l(t2n), _l(dstn), scale=-1.0)

    dma("sync", IDENTB[:], identb_d, [], ["identb"])
    dma("sync", IDENTF[:], identf_d, [], ["identf"])
    dma("sync", TRIB[:], trib_d, [], ["trib"])
    dma("sync", IOTA[:], iota16_d.broadcast_to((128, 16)), [], ["iota"])
    dma("sync", ONESB[:], ones128b_d, [], ["onesb"])
    dma("sync", ONESK1[:], onesK1f_d, [], ["onesk1"])
    dma("sync", HALFON[:], halfones_d, [], ["halfones"])
    vop(lambda v: v.memset(ONESF[:], 1.0), [], ["onesf"])
    vop(lambda v: v.memset(ONESH[:], 0.5), [], ["ones2"])
    dma("sync", THREP, threp_d, [], ["s0"], no_waw=True)
    dma("sync", ALREP, alrep_d, [], ["s0"], no_waw=True)
    dma("sync", ETREP, etrep_d, [], ["s0"], no_waw=True)
    dma("sync", KNREP, knorm.broadcast_to((128, D)), [], ["s0"], no_waw=True)
    dma("sync", QNREP, qnorm.broadcast_to((128, D)), [], ["s0"], no_waw=True)
    for c in range(8):
        dma("gpsimd", WKT[:, c * 1024:(c + 1) * 1024], wkT[c * 128:(c + 1) * 128, :],
            [], ["s1"], stream="ldg", no_waw=True)
        dma("gpsimd", WVT[:, c * 1024:(c + 1) * 1024], wvT[c * 128:(c + 1) * 128, :],
            [], ["w1tw"], stream="ldg", no_waw=True)
        dma("gpsimd", WQT[:, c * 1024:(c + 1) * 1024], wqT[c * 128:(c + 1) * 128, :],
            [], ["s1"], stream="ldg", no_waw=True)
    vop(lambda v: v.memset(ZPAN, 0.0), ["s0"], ["zpan"])

    def transpose_tile(dst_ap, dstn, src_ap, src_names, identap, identn, f32):
        pc = psCf if f32 else psC
        for h in range(2):
            for c4 in range(4):
                cc = h * 4 + c4
                transp(pc[:, c4 * 128:(c4 + 1) * 128],
                       src_ap[:, cc * 128:(cc + 1) * 128], identap, identn,
                       rd=src_names, wr=["psC"])
            vop(lambda v, h=h: v.tensor_copy(out=dst_ap[:, h * 512:(h + 1) * 512],
                                             in_=pc[:, 0:512]), ["psC"], [dstn])

    def projection(dst_name, dst_ap, wt_ap, wt_name, nrm_rep, do_norm):
        for c in range(8):
            for n in range(2):
                mm(psA[:, n * 512:(n + 1) * 512],
                   XTT_A[:, c * 128:(c + 1) * 128],
                   wt_ap[:, c * 1024 + n * 512: c * 1024 + (n + 1) * 512],
                   start=(c == 0), stop=(c == 7),
                   rd=[wt_name, "ht"], wr=["psA"])
        pa = psA[:, 0:1024]
        sigtriple(SGA, "sga", pa, "psA", -1.0, T1A, "bigf", TRASH[:, 0:1024],
                  "trash")
        if not do_norm:
            vop(lambda v: v.scalar_tensor_tensor(out=dst_ap, in0=pa, scalar=1.0,
                                                 in1=SGA, op0=ALU.mult,
                                                 op1=ALU.mult),
                ["psA", "sga"], [dst_name])
        else:
            vop(lambda v: v.scalar_tensor_tensor(out=KP_A, in0=pa, scalar=1.0,
                                                 in1=SGA, op0=ALU.mult,
                                                 op1=ALU.mult),
                ["psA", "sga"], ["bigf"])
            act(TRASH[:, 0:1024], KP_A, AF.Square, ["bigf"], ["trash", "rc"],
                accum=RC[:, 4:5])
            vop(lambda v: v.tensor_scalar(out=RC[:, 5:6], in0=RC[:, 4:5],
                                          scalar1=1.0 / D, scalar2=EPS,
                                          op0=ALU.mult, op1=ALU.add), ["rc"],
                ["rc"])
            act(RC[:, 5:6], RC[:, 5:6], AF.Ln, ["rc"], ["rc"])
            act(RC[:, 5:6], RC[:, 5:6], AF.Exp, ["rc"], ["rc"], scale=-0.5)
            vop(lambda v: v.scalar_tensor_tensor(out=dst_ap, in0=KP_A,
                                                 scalar=RC[:, 5:6], in1=nrm_rep,
                                                 op0=ALU.mult, op1=ALU.mult),
                ["bigf", "rc", "s0"], [dst_name])

    def phaseA_tile(ti, qonly, src, li=None):
        par = ti % 2
        xt = XT_A[:, par * 1024:(par + 1) * 1024]
        xname = f"xA{par}"
        dma("sync", xt, src, [], [xname, "bigf"], stream="ldx")
        transpose_tile(XTT_A, "ht", xt, [xname, "bigf"], IDENTF[:], "identf", True)
        if not qonly:
            bb = ti // NTB
            cpair = 2 * (ti % NTB)
            projection("ktA", KDST_A, WKT, "s1", KNREP, True)
            projection("vtA", VT_A, WVT, "w1tw", None, False)
            vop(lambda v: v.tensor_sub(out=VT_A, in0=KDST_A, in1=VT_A),
                ["ktA", "vtA"], ["vtA", "da"])
            for half in range(2):
                rows = slice(half * 64, half * 64 + 64)
                cch = cpair + half
                dma("scalar", ktok_d[cch, bb * 64:(bb + 1) * 64, :],
                    KDST_A[rows, :], ["ktA"], ["ktokd"], stream="stK",
                    no_waw=True)
                dma("scalar", kmv_d[cch, bb * 64:(bb + 1) * 64, :], VT_A[rows, :],
                    ["vtA"], ["kmvd"], stream="stM", no_waw=True)
            for wi, wrep in enumerate((THREP, ALREP, ETREP)):
                vop(lambda v, wrep=wrep, wi=wi, xt=xt, ti=ti:
                    v.scalar_tensor_tensor(
                        out=TRASHF, in0=xt, scalar=1.0, in1=wrep, op0=ALU.mult,
                        op1=ALU.mult,
                        accum_out=ZPAN[:, wi * NTILE_G + ti: wi * NTILE_G + ti + 1]),
                    [xname, "s0", "zpan", "bigf"], ["zpan", "trash"])
        else:
            projection("qtA", QT_A, WQT, "s1", QNREP, True)
            dma("scalar", qtok_d[li * 128:(li + 1) * 128, :], QT_A,
                ["qtA", "da"], ["qtokd"], stream="stQ", no_waw=True)

    for ti in range(NTILE_G):
        phaseA_tile(ti, False, x_full[ti * 128:(ti + 1) * 128, :])
    for li in range(NTILE_L):
        phaseA_tile(li, True, x_loc[li * 128:(li + 1) * 128, :], li=li)

    mm(psB[0:2, 0:3 * NTILE_G], HALFON[:], ZPAN, True, True, ["zpan", "halfones"], ["psB"])
    vop(lambda v: v.tensor_copy(out=ZSB, in_=psB[0:2, 0:3 * NTILE_G]), ["psB"], ["pan"])
    dma("scalar", zd_d, ZSB, ["pan"], ["zdd"], stream="stA")
    for wi, dst in enumerate((THsb, ALsb, ETsb)):
        src = zd_d[:, wi * NTILE_G:(wi + 1) * NTILE_G]
        src3 = src.rearrange("h (b cc) -> b cc h", b=2)
        dma("scalar", dst.rearrange("b (cc h) -> b cc h", cc=NTB), src3,
            ["zdd"], ["pan"], stream="ldpanS", no_waw=True,
            allow_slow_non_contiguous=True)
    sigtriple(TH2, "pan", THsb, "pan", -1.0, BIGF[0:2, 0:NC], "bigf",
              TRASH[0:2, 0:NC], "trash")
    vop(lambda v: v.tensor_scalar(out=TH2, in0=TH2, scalar1=BASE_LR,
                                  scalar2=None, op0=ALU.mult), ["pan"], ["pan"])
    dma("scalar", th_d, TH2, ["pan"], ["thd"], stream="stA")
    mm(psB[0:1, 0:NC], ONESH[32:34, :], ETsb, True, True, ["pan", "ones2"], ["psB"])
    mm(psB[0:1, NC:2 * NC], ONESH[32:34, :], ALsb, True, True, ["pan", "ones2"], ["psB"])
    vop(lambda v: v.tensor_copy(out=EB2, in_=psB[0:1, 0:2 * NC]), ["psB"], ["pan"])
    sigtriple(EB2, "pan", EB2, "pan", -1.0, BIGF[0:1, 0:2 * NC], "bigf",
              TRASH[0:1, 0:2 * NC], "trash")
    vop(lambda v: v.tensor_scalar(out=EB2[0:1, NC:2 * NC],
                                  in0=EB2[0:1, NC:2 * NC], scalar1=-1.0,
                                  scalar2=1.0, op0=ALU.mult, op1=ALU.add),
        ["pan"], ["pan"])
    dma("scalar", etab_d[0:1, :], EB2[0:1, 0:NC], ["pan"], ["etabd"], stream="stA", no_waw=True)
    dma("scalar", etab_d[1:2, :], EB2[0:1, NC:2 * NC], ["pan"], ["etabd"], stream="stA", no_waw=True)
    dma("sync", THCOL[0:64, :], th_d[0:1, :].broadcast_to((64, NC)), ["thd"],
        ["pan"], stream="ldpan", no_waw=True)
    dma("sync", THCOL[64:128, :], th_d[1:2, :].broadcast_to((64, NC)), ["thd"],
        ["pan"], stream="ldpan", no_waw=True)
    dma("sync", ETACOL, etab_d[0:1, :].broadcast_to((128, NC)), ["etabd"], ["pan"], stream="ldpan", no_waw=True)
    dma("sync", BETACOL, etab_d[1:2, :].broadcast_to((128, NC)), ["etabd"], ["pan"], stream="ldpan", no_waw=True)

    if tapt("theta", [2, NC]) is not None:
        dma("scalar", tap_out["theta"], th_d, ["thd"], ["tap_theta"], stream="tap", no_waw=True)
    if tapt("etab", [2, NC]) is not None:
        dma("scalar", tap_out["etab"], etab_d, ["etabd"], ["tap_etab"], stream="tap", no_waw=True)
    if tapt("ktok", [NC, 128, D], BF16) is not None:
        dma("scalar", tap_out["ktok"], ktok_d, ["ktokd"], ["tap_ktok"], stream="tap", no_waw=True)
    if tapt("kmv", [NC, 128, D], BF16) is not None:
        dma("scalar", tap_out["kmv"], kmv_d, ["kmvd"], ["tap_kmv"], stream="tap", no_waw=True)
    if tapt("qtok", [TLOC, D], BF16) is not None:
        dma("scalar", tap_out["qtok"], qtok_d, ["qtokd"], ["tap_qtok"], stream="tap", no_waw=True)

    def stop_here():
        sch.final_waits = [(k, v) for k, v in sch.cum.items() if v > 0]

    if STOP_AFTER == "A":
        stop_here()
    W0N = [f"W0T_{i}" for i in range(8)]
    W1NN = [f"W1N_{i}" for i in range(8)]
    S0N = [f"S0_{i}" for i in range(8)]
    S1N = [f"S1_{i}" for i in range(8)]
    PSAF = ["psA0", "psA1", "psA2", "psA3"]
    PSBF = ["psB0", "psB1"]
    KTOK2 = GSB[:, 2048:3072]
    KMV2 = GSB[:, 3072:4096]
    GDUMP = GSB[:, 0:2048]
    sch.fence()
    for c in range(8):
        dma("gpsimd", W0T[:, c * INTER:(c + 1) * INTER],
            w0T_d[c * 128:(c + 1) * 128, :], [], [f"W0T_{c}"], stream="ldg",
            no_waw=True)
        dma("gpsimd", W1N[:, c * INTER:(c + 1) * INTER],
            w1n_d[c * 128:(c + 1) * 128, :], [], [f"W1N_{c}"], stream="ldg",
            no_waw=True)
    dma("gpsimd", w1n_sc.rearrange("(c p) f -> p c f", p=128),
        w1n_d.rearrange("(c p) f -> p c f", p=128), [], ["w1nsc"],
        stream="w1ns")
    vop(lambda v: v.memset(S0[:], 0.0), ["s0"], S0N)
    vop(lambda v: v.memset(S1[:], 0.0), ["s1"], S1N)
    vop(lambda v: v.memset(NORMP, 0.0), ["pan"], ["pan"])
    vop(lambda v: v.memset(SP[:], 0.0), [], ["sp"])
    vop(lambda v: v.memset(SR[:], 0.0), [], ["sr"])
    vop(lambda v: v.memset(SP[:, 0:1], 1.0), [], ["sp"])
    vop(lambda v: v.memset(SR[0:1, 1:2], 1.0), [], ["sr"])
    dma("sync", LNMF[:], lnw_d.broadcast_to((128, D)), [], ["lnmf"])
    vop(lambda v: v.memset(SLNR[:], 0.0), [], ["sln"])

    def load_w1t_chunk(ic, slot):
        dma("sync", W1TW[:, slot * 1024:(slot + 1) * 1024],
            w1n_sc[:, ic * 128:(ic + 1) * 128], ["w1nsc"], [f"w1tw{slot}"],
            stream=f"w1t{slot}", transpose=True)

    for ic in range(8):
        load_w1t_chunk(ic, ic)

    cast_jobs = []
    for e in range(NEXP):
        cast_jobs.append((W1b_d[e].rearrange("d f -> (d f)"),
                          W1f_d[e].rearrange("d f -> (d f)"), f"wc1_{e}"))
        cast_jobs.append((W2b_d[e].rearrange("f d -> (f d)"),
                          W2f_d[e].rearrange("f d -> (f d)"), f"wc2_{e}"))

    def issue_cast(i):
        if 0 <= i < len(cast_jobs):
            dst, src, nm = cast_jobs[i]
            dma("gpsimd", dst.rearrange("(a b) -> a b", a=2048),
                src.rearrange("(a b) -> a b", a=2048), [], [nm], stream="cast")

    dma("gpsimd", eb_d, zeb_d, [], ["eb"], stream="ebs")
    dma("gpsimd", ebo_d, zebo_d, [], ["ebod"], stream="ebg")
    dma("sync", KTOK[:], ktok_d[0], ["ktokd"], ["ktok"], stream="ktok")
    dma("sync", KMV[:], kmv_d[0], ["kmvd"], ["kmv"], stream="kmv")

    def scan_step(c):
        par = c % 2
        ktok = KTOK[:] if par == 0 else KTOK2
        kmv = KMV[:] if par == 0 else KMV2
        kt_n = "ktok" if par == 0 else "ktokB"
        kmv_n = "kmv" if par == 0 else "kmvB"
        ktok_nx = KTOK2 if par == 0 else KTOK[:]
        ktn_nx = "ktokB" if par == 0 else "ktok"
        for _j in range(c, len(cast_jobs), NC):
            issue_cast(_j)
        if c + 1 < NC:
            dma("sync", ktok_nx, ktok_d[c + 1], [], [ktn_nx], stream="ktok")
            dma("sync", KMV2 if par == 0 else KMV[:], kmv_d[c + 1], [],
                ["kmvB" if par == 0 else "kmv"], stream="kmv")
        vop(lambda v: v.memset(NORMP, 0.0), ["pan"], ["pan"])
        if c == 0:
            for bq in range(8):
                transp(psC[:, bq * 128:(bq + 1) * 128],
                       ktok[:, bq * 128:(bq + 1) * 128],
                       IDENTB[:], "identb", rd=[kt_n], wr=["psC"])
            vop(lambda v: v.tensor_copy(out=KT, in_=psC[:, 0:1024]), ["psC"],
                ["trash"])
        vop(lambda v: v.tensor_scalar(out=SP[:, 3:4], in0=SP[:, 0:1], scalar1=-1.0,
                                      scalar2=None, op0=ALU.mult), ["sp"], ["sp"])
        for kk in range(8):
            for n in range(4):
                mm(psA[:, n * 512:(n + 1) * 512], KT[:, kk * 128:(kk + 1) * 128],
                   W0T[:, kk * INTER + n * 512: kk * INTER + (n + 1) * 512],
                   start=(kk == 0), stop=(kk == 7), rd=["trash", f"W0T_{kk}"],
                   wr=[f"psA{n}"])
        sigtriple(SG, "bigf", psA[:], PSAF, SP[:, 3:4], UB, "bigf", DA[:],
                  "da", extra_rd=["sp"])
        vop(lambda v: v.scalar_tensor_tensor(out=HTOK[:], in0=psA[:],
                                             scalar=SP[:, 0:1], in1=SG,
                                             op0=ALU.mult, op1=ALU.mult),
            PSAF + ["bigf", "sp"], ["htok"])
        for bq in range(16):
            transp(psC[:, bq * 128:(bq + 1) * 128],
                   HTOK[:, bq * 128:(bq + 1) * 128],
                   IDENTB[:], "identb", rd=["htok"], wr=["psC"])
        act(HT[:], psC[:], AF.Copy, ["psC"], ["ht"])
        vop(lambda v: v.tensor_mul(out=UB, in0=HTOK[:], in1=SG), ["htok", "bigf"],
            ["bigf"])
        vop(lambda v: v.scalar_tensor_tensor(out=UB, in0=UB, scalar=-1.0,
                                             in1=SG, op0=ALU.mult, op1=ALU.add),
            ["bigf"], ["bigf"])
        vop(lambda v: v.tensor_tensor(out=UB, in0=UB, in1=HTOK[:], op=ALU.add),
            ["bigf", "htok"], ["bigf"])
        for ic in range(16):
            slot = ic % 8
            for n in range(2):
                mm(psB[:, n * 512:(n + 1) * 512], HT[:, ic * 128:(ic + 1) * 128],
                   W1TW[:, slot * 1024 + n * 512: slot * 1024 + (n + 1) * 512],
                   start=(ic == 0), stop=(ic == 15),
                   rd=["ht", f"w1tw{slot}"], wr=[f"psB{n}"])
            if ic < 8:
                load_w1t_chunk(ic + 8, slot)
        act(GSBf[:, 0:1024], psB[:], AF.Square, PSBF, ["gdf", "dat", "rc"],
            accum=RC[:, 4:5])
        vop(lambda v: v.tensor_scalar(out=RC[:, 6:7], in0=SP[:, 0:1],
                                      scalar1=SP[:, 0:1], scalar2=1.0 / D,
                                      op0=ALU.mult, op1=ALU.mult), ["sp"], ["rc"])
        vop(lambda v: v.tensor_scalar(out=RC[:, 5:6], in0=RC[:, 4:5],
                                      scalar1=RC[:, 6:7], scalar2=EPS,
                                      op0=ALU.mult, op1=ALU.add), ["rc"], ["rc"])
        act(RC[:, 5:6], RC[:, 5:6], AF.Ln, ["rc"], ["rc"])
        act(RC[:, 0:1], RC[:, 5:6], AF.Exp, ["rc"], ["rc"], scale=-0.5)
        vop(lambda v: v.tensor_mul(out=T0[:], in0=psB[:], in1=LNMF[:]),
            PSBF + ["lnmf"], ["t0"])
        vop(lambda v: v.tensor_scalar(out=RC[:, 6:7], in0=RC[:, 0:1],
                                      scalar1=SP[:, 0:1], scalar2=None,
                                      op0=ALU.mult), ["rc", "sp"], ["rc"])
        vop(lambda v, kmv=kmv: v.scalar_tensor_tensor(out=E0[:], in0=T0[:],
                                                      scalar=RC[:, 6:7], in1=kmv,
                                                      op0=ALU.mult, op1=ALU.add),
            ["t0", "rc", kmv_n], ["e0"])
        vop(lambda v: v.tensor_mul(out=T1[:], in0=E0[:], in1=LNMF[:]),
            ["e0", "lnmf"], ["t1"])
        vop(lambda v: v.scalar_tensor_tensor(out=GSBf[:, 0:1024], in0=psB[:],
                                             scalar=1.0,
                                             in1=T1[:], op0=ALU.mult,
                                             op1=ALU.mult, accum_out=RC[:, 1:2]),
            PSBF + ["t1"], ["gdf", "rc"])
        vop(lambda v, c=c: v.tensor_scalar(out=RC[:, 2:3], in0=THCOL[:, c:c + 1],
                                           scalar1=RC[:, 0:1], scalar2=KAPPA,
                                           op0=ALU.mult, op1=ALU.mult),
            ["pan", "rc"], ["rc"])
        vop(lambda v: v.tensor_scalar(out=RC[:, 6:7], in0=RC[:, 6:7],
                                      scalar1=RC[:, 6:7], scalar2=1.0 / D,
                                      op0=ALU.mult, op1=ALU.mult), ["rc"], ["rc"])
        vop(lambda v: v.tensor_scalar(out=RC[:, 3:4], in0=RC[:, 2:3],
                                      scalar1=RC[:, 6:7], scalar2=None,
                                      op0=ALU.mult), ["rc"], ["rc"])
        vop(lambda v: v.tensor_scalar(out=RC[:, 3:4], in0=RC[:, 3:4],
                                      scalar1=RC[:, 1:2], scalar2=None,
                                      op0=ALU.mult), ["rc"], ["rc"])
        vop(lambda v: v.tensor_scalar(out=GSBf[:, 0:1024], in0=psB[:],
                                      scalar1=RC[:, 3:4],
                                      scalar2=None, op0=ALU.mult),
            PSBF + ["rc"], ["gdf"])
        vop(lambda v: v.scalar_tensor_tensor(out=DY[:], in0=T1[:],
                                             scalar=RC[:, 2:3],
                                             in1=GSBf[:, 0:1024],
                                             op0=ALU.mult, op1=ALU.subtract),
            ["t1", "rc", "gdf"], ["dy"])
        vop(lambda v: v.tensor_scalar(out=RC[:, 6:7], in0=RC[:, 2:3],
                                      scalar1=SP[:, 0:1], scalar2=None,
                                      op0=ALU.mult), ["rc", "sp"], ["rc"])
        vop(lambda v: v.scalar_tensor_tensor(out=ZLN[:], in0=psB[:],
                                             scalar=RC[:, 6:7], in1=E0[:],
                                             op0=ALU.mult, op1=ALU.mult),
            PSBF + ["rc", "e0", "t0"], ["t0"])
        for bq in range(8):
            transp(psC[:, bq * 128:(bq + 1) * 128], DY[:, bq * 128:(bq + 1) * 128],
                   IDENTB[:], "identb", rd=["dy"], wr=["psC"])
        act(DYT[:], psC[:, 0:1024], AF.Copy, ["psC"], ["dyt", "e0"])
        for n in range(2):
            mm(psB[0:1, n * 512:(n + 1) * 512], ONESB[:],
               ZLN[:, n * 512:(n + 1) * 512], True, True, rd=["t0", "onesb"],
               wr=[f"psB{n}"])
        for kk in range(8):
            for n in range(4):
                mm(psA[:, n * 512:(n + 1) * 512], DYT[:, kk * 128:(kk + 1) * 128],
                   W1N[:, kk * INTER + n * 512: kk * INTER + (n + 1) * 512],
                   start=(kk == 0), stop=(kk == 7), rd=["dyt", f"W1N_{kk}"],
                   wr=[f"psA{n}"])
        vop(lambda v: v.scalar_tensor_tensor(out=DA[:], in0=psA[:],
                                             scalar=SP[:, 0:1], in1=UB,
                                             op0=ALU.mult, op1=ALU.mult),
            PSAF + ["bigf", "sp"], ["da"])
        for bq in range(16):
            transp(psC[:, bq * 128:(bq + 1) * 128], DA[:, bq * 128:(bq + 1) * 128],
                   IDENTB[:], "identb", rd=["da"], wr=["psC"])
        act(GDUMP, psC[:], AF.Copy, ["psC"], ["dat", "gdf"])
        for kk in range(8):
            mm(psCf[:, 0:128], DYT[:, kk * 128:(kk + 1) * 128],
               DYT[:, kk * 128:(kk + 1) * 128], start=(kk == 0), stop=(kk == 7),
               rd=["dyt", "dat"], wr=["psC"])
        for ic in range(16):
            mm(psCf[:, 128:256], HT[:, ic * 128:(ic + 1) * 128],
               HT[:, ic * 128:(ic + 1) * 128], start=(ic == 0), stop=(ic == 15),
               rd=["ht"], wr=["psC"])
        for kk in range(8):
            mm(psCf[:, 512:640], KT[:, kk * 128:(kk + 1) * 128],
               KT[:, kk * 128:(kk + 1) * 128], start=(kk == 0), stop=(kk == 7),
               rd=["trash"], wr=["psC"])
        for ic in range(16):
            mm(psCf[:, 640:768], GDUMP[:, ic * 128:(ic + 1) * 128],
               GDUMP[:, ic * 128:(ic + 1) * 128], start=(ic == 0), stop=(ic == 15),
               rd=["dat"], wr=["psC"])
        vop(lambda v: v.tensor_copy(out=BIGF[:, 0:128], in_=psCf[:, 0:128]),
            ["psC"], ["gram0"])
        vop(lambda v: v.tensor_copy(out=BIGF[:, 128:256], in_=psCf[:, 512:640]),
            ["psC"], ["gram1"])
        vop(lambda v: v.scalar_tensor_tensor(out=BIGF[:, 256:384],
                                             in0=psCf[:, 128:256], scalar=1.0,
                                             in1=BIGF[:, 0:128], op0=ALU.mult,
                                             op1=ALU.mult,
                                             accum_out=NORMP[:, 0:1]),
            ["psC", "gram0", "pan"], ["gram2", "pan"])
        vop(lambda v: v.scalar_tensor_tensor(out=BIGF[:, 384:512],
                                             in0=psCf[:, 640:768], scalar=1.0,
                                             in1=BIGF[:, 128:256], op0=ALU.mult,
                                             op1=ALU.mult,
                                             accum_out=NORMP[:, 16:17]),
            ["psC", "gram1", "pan"], ["gram2", "pan"])
        act(BIGF[0:1, 1024:2048], psB[0:1, 0:1024], AF.Square, PSBF,
            ["bigf", "pan"], accum=NORMP[0:1, 32:33])
        mm(psCf[0:1, 0:34], ONESF[:], NORMP[:, 0:34], True, True,
           rd=["pan", "onesf"], wr=["psC"])
        vop(lambda v: v.reduce_sum(out=SR[0:1, 4:5], in_=psCf[0:1, 0:34],
                                   axis=AX.X), ["psC"], ["sr"])
        vop(lambda v: v.tensor_scalar(out=SR[0:1, 4:5], in0=SR[0:1, 4:5],
                                      scalar1=1e-38, scalar2=None, op0=ALU.add),
            ["sr"], ["sr"])
        act(SR[0:1, 4:5], SR[0:1, 4:5], AF.Ln, ["sr"], ["sr"])
        act(SR[0:1, 4:5], SR[0:1, 4:5], AF.Exp, ["sr"], ["sr"], scale=0.5)
        vop(lambda v: v.tensor_scalar(out=SR[0:1, 0:1], in0=SR[0:1, 4:5],
                                      scalar1=1e-6, scalar2=None, op0=ALU.add),
            ["sr"], ["sr"])
        vop(lambda v: v.reciprocal(out=SR[0:1, 0:1], in_=SR[0:1, 0:1]), ["sr"],
            ["sr"])
        vop(lambda v: v.tensor_scalar(out=SR[0:1, 0:1], in0=SR[0:1, 0:1],
                                      scalar1=1.0, scalar2=None, op0=ALU.min),
            ["sr"], ["sr"])
        vop(lambda v: v.reciprocal(out=SR[0:1, 5:6], in_=SR[0:1, 0:1]), ["sr"],
            ["sr"])
        vop(lambda v, c=c: v.tensor_scalar(out=SR[0:1, 2:3],
                                           in0=ETACOL[0:1, c:c + 1],
                                           scalar1=SR[0:1, 1:2],
                                           scalar2=SR[0:1, 5:6],
                                           op0=ALU.mult, op1=ALU.mult),
            ["pan", "sr"], ["sr"])
        vop(lambda v, c=c: v.tensor_scalar(out=SR[0:1, 3:4],
                                           in0=BETACOL[0:1, c:c + 1],
                                           scalar1=SR[0:1, 1:2],
                                           scalar2=SR[0:1, 5:6],
                                           op0=ALU.mult, op1=ALU.mult),
            ["pan", "sr"], ["sr"])
        vop(lambda v: v.tensor_copy(out=SR[0:1, 1:2], in_=SR[0:1, 0:1]), ["sr"],
            ["sr"])
        mm(psCf[:, 0:4], ONESK1[:], SR[0:1, 0:4], True, True, rd=["sr", "onesk1"],
           wr=["psC"])
        vop(lambda v: v.tensor_copy(out=SP[:, 0:1], in_=psCf[:, 0:1]), ["psC"],
            ["sp"])
        vop(lambda v: v.tensor_copy(out=SP[:, 1:2], in_=psCf[:, 2:3]), ["psC"],
            ["sp"])
        vop(lambda v: v.tensor_copy(out=SP[:, 2:3], in_=psCf[:, 3:4]), ["psC"],
            ["sp"])
        for m in range(8):
            act(W1N[:, m * INTER:(m + 1) * INTER],
                W1N[:, m * INTER:(m + 1) * INTER], AF.Copy,
                ["sp", f"W1N_{m}"], [f"W1N_{m}"], scale=SP[:, 2:3])

        def grad_apply(lhs_src, lhs_n, rhs_ap, rhs_n, Stile, Spfx, Wtile, Wpfx,
                       pool_w, wb):
            for m in range(8):
                for s in range(4):
                    mm(psA[:, s * 512:(s + 1) * 512],
                       lhs_src[:, m * 128:(m + 1) * 128],
                       rhs_ap[:, s * 512:(s + 1) * 512],
                       True, True, rd=[lhs_n, rhs_n], wr=[f"psA{s}"])
                for h in range(2):
                    vop(lambda v, m=m, h=h: v.scalar_tensor_tensor(
                        out=Stile[:, m * INTER + h * 1024:
                                  m * INTER + h * 1024 + 1024],
                        in0=Stile[:, m * INTER + h * 1024:
                                  m * INTER + h * 1024 + 1024],
                        scalar=SP[:, 1:2],
                        in1=psA[:, h * 1024:(h + 1) * 1024],
                        op0=ALU.mult, op1=ALU.subtract),
                        [f"psA{2 * h}", f"psA{2 * h + 1}", "sp", f"{Spfx}{m}"],
                        [f"{Spfx}{m}"])
                if pool_w:
                    sch.op("gpsimd",
                           lambda g, m=m: g.tensor_tensor(
                               out=Wtile[:, m * INTER:(m + 1) * INTER],
                               in0=Wtile[:, m * INTER:(m + 1) * INTER],
                               in1=Stile[:, m * INTER:(m + 1) * INTER],
                               op=ALU.add),
                           reads=[f"{Spfx}{m}", f"{Wpfx}{m}"],
                           writes=[f"{Wpfx}{m}"])
                else:
                    vop(lambda v, m=m: v.scalar_tensor_tensor(
                        out=Wtile[:, m * INTER:(m + 1) * INTER],
                        in0=Wtile[:, m * INTER:(m + 1) * INTER],
                        scalar=SP[:, 2:3],
                        in1=Stile[:, m * INTER:(m + 1) * INTER],
                        op0=ALU.mult, op1=ALU.add),
                        ["sp", f"{Spfx}{m}", f"{Wpfx}{m}"], [f"{Wpfx}{m}"])
                if wb:
                    dma("gpsimd", w1n_sc[m * 128:(m + 1) * 128, :],
                        Wtile[:, m * INTER:(m + 1) * INTER], [f"{Wpfx}{m}"],
                        ["w1nsc"], stream="w1ns", no_waw=True)
        grad_apply(DY, "dy", HTOK[:], "htok", S1, "S1_", W1N, "W1N_",
                   True, True)
        grad_apply(ktok, kt_n, DA[:], "da", S0, "S0_", W0T, "W0T_",
                   False, False)
        if c + 1 < NC:
            for ic in range(8):
                load_w1t_chunk(ic, ic)
            for bq in range(8):
                transp(psC[:, bq * 128:(bq + 1) * 128],
                       ktok_nx[:, bq * 128:(bq + 1) * 128],
                       IDENTB[:], "identb", rd=[ktn_nx], wr=["psC"])
            vop(lambda v: v.tensor_copy(out=KT, in_=psC[:, 0:1024]), ["psC"],
                ["trash"])
        vop(lambda v: v.tensor_scalar(out=BIGF[0:1, 0:1024], in0=psB[0:1, 0:1024],
                                      scalar1=SR[0:1, 0:1], scalar2=None,
                                      op0=ALU.mult), PSBF + ["sr"], ["bigf"])
        for n in range(2):
            mm(psCf[:, n * 512:(n + 1) * 512], ONESK1[:],
               BIGF[0:1, n * 512:(n + 1) * 512], True, True,
               rd=["bigf", "onesk1"], wr=["psC"])
        vop(lambda v, c=c: v.scalar_tensor_tensor(out=SLNR[:], in0=SLNR[:],
                                                  scalar=ETACOL[:, c:c + 1],
                                                  in1=psCf[:, 0:1024],
                                                  op0=ALU.mult, op1=ALU.subtract),
            ["psC", "pan", "sln"], ["sln"])
        vop(lambda v, c=c: v.scalar_tensor_tensor(out=LNMF[:], in0=LNMF[:],
                                                  scalar=BETACOL[:, c:c + 1],
                                                  in1=SLNR[:], op0=ALU.mult,
                                                  op1=ALU.add),
            ["sln", "pan", "lnmf"], ["lnmf"])
        nm = f"state{c}"
        if nm in taps:
            tw = tapt(nm, [4, 128, 8 * INTER], BF16)
            for i, (tile, tn) in enumerate(((W0T, W0N), (W1N, W1NN),
                                            (S0, S0N), (S1, S1N))):
                dma("scalar", tw[i], tile[:], tn, [f"tap_{nm}"],
                    stream="tap", no_waw=True)
        if nm + "_sc" in taps:
            tw2 = tapt(nm + "_sc", [1, 8])
            dma("scalar", tw2, SR[0:1, :], ["sr"], [f"tap_{nm}s"],
                stream="tap", no_waw=True)
        if nm + "_ln" in taps:
            tw3 = tapt(nm + "_ln", [1, D])
            dma("scalar", tw3, LNMF[0:1, :], ["lnmf"], [f"tap_{nm}l"],
                stream="tap", no_waw=True)

    if STOP_AFTER != "A":
        for c in range(NC):
            scan_step(c if STOP_AFTER != "prep" else c)
    if STOP_AFTER in ("prep", "scan"):
        stop_here()
    sch.fence()
    W1TF = S0[:]
    for ic in range(16):
        dma("scalar", W1TF[:, ic * 1024:(ic + 1) * 1024],
            w1n_sc[:, ic * 128:(ic + 1) * 128], ["w1nsc"], ["w1tf"],
            stream="w1tC", no_waw=True, transpose=True)



    def layernorm_tile(z_ap, zn, outn, extran=()):
        ex = list(extran)
        vop(lambda v: v.reduce_sum(out=RC[:, 4:5], in_=z_ap, axis=AX.X),
            [zn] + ex, ["rc"])
        vop(lambda v: v.tensor_scalar(out=RC[:, 4:5], in0=RC[:, 4:5],
                                      scalar1=1.0 / D, scalar2=None,
                                      op0=ALU.mult), ["rc"], ["rc"])
        act(TRASH[:, 0:1024], z_ap, AF.Square, [zn] + ex, ["trash", "rc"],
            accum=RC[:, 5:6])
        vop(lambda v: v.tensor_scalar(out=RC[:, 6:7], in0=RC[:, 4:5],
                                      scalar1=RC[:, 4:5], scalar2=-1.0,
                                      op0=ALU.mult, op1=ALU.mult), ["rc"], ["rc"])
        vop(lambda v: v.scalar_tensor_tensor(out=RC[:, 5:6], in0=RC[:, 5:6],
                                             scalar=1.0 / D, in1=RC[:, 6:7],
                                             op0=ALU.mult, op1=ALU.add),
            ["rc"], ["rc"])
        vop(lambda v: v.tensor_scalar(out=RC[:, 5:6], in0=RC[:, 5:6],
                                      scalar1=LN_EPS, scalar2=1e-30, op0=ALU.add,
                                      op1=ALU.max), ["rc"], ["rc"])
        vop(lambda v: v.tensor_scalar(out=RC[:, 5:6], in0=RC[:, 5:6],
                                      scalar1=1e30, scalar2=None, op0=ALU.min),
            ["rc"], ["rc"])
        act(RC[:, 5:6], RC[:, 5:6], AF.Ln, ["rc"], ["rc"])
        act(RC[:, 5:6], RC[:, 5:6], AF.Exp, ["rc"], ["rc"], scale=-0.5)
        vop(lambda v: v.tensor_scalar(out=z_ap, in0=z_ap, scalar1=RC[:, 4:5],
                                      scalar2=None, op0=ALU.subtract),
            [zn, "rc"] + ex, [zn] + ex)
        vop(lambda v: v.tensor_scalar(out=z_ap, in0=z_ap, scalar1=RC[:, 5:6],
                                      scalar2=None, op0=ALU.mult), [zn, "rc"] + ex,
            [outn] + ex)

    def phaseC_tile(li):
        dma("sync", QTOK_C, qtok_d[li * 128:(li + 1) * 128, :],
            ["qtokd"] if li == 0 else [], ["qc"], stream="ldq")
        dma("sync", XL_C, x_loc[li * 128:(li + 1) * 128, :], [], ["xl", "bigf"], stream="ldxl")
        qn, xn = "qc", "xl"
        qt = QTOK_C
        for bq in range(8):
            transp(psC[:, bq * 128:(bq + 1) * 128], qt[:, bq * 128:(bq + 1) * 128],
                   IDENTB[:], "identb", rd=[qn], wr=["psC"])
        vop(lambda v: v.tensor_copy(out=KT, in_=psC[:, 0:1024]), ["psC"], ["trash"])
        for kk in range(8):
            for n in range(4):
                mm(psA[:, n * 512:(n + 1) * 512], KT[:, kk * 128:(kk + 1) * 128],
                   W0T[:, kk * INTER + n * 512: kk * INTER + (n + 1) * 512],
                   start=(kk == 0), stop=(kk == 7), rd=["trash"] + W0N,
                   wr=["psA"])
        vop(lambda v: v.tensor_scalar(out=SP[:, 3:4], in0=SP[:, 0:1], scalar1=-1.0,
                                      scalar2=None, op0=ALU.mult), ["sp"], ["sp"])
        sigtriple(GSBf, "gsb0", psA[:], "psA", SP[:, 3:4], GSBf, "gsb0",
                  TRASH[:], "trash", extra_rd=["sp"])
        vop(lambda v: v.scalar_tensor_tensor(out=HTOK[:], in0=psA[:],
                                             scalar=SP[:, 0:1], in1=GSBf,
                                             op0=ALU.mult, op1=ALU.mult),
            ["psA", "gsb0", "sp"], ["htok"])
        for bq in range(16):
            transp(psC[:, bq * 128:(bq + 1) * 128],
                   HTOK[:, bq * 128:(bq + 1) * 128],
                   IDENTB[:], "identb", rd=["htok"], wr=["psC"])
        vop(lambda v: v.tensor_copy(out=HT[:], in_=psC[:]), ["psC"], ["ht"])
        for ic in range(16):
            for n in range(2):
                mm(psB[:, n * 512:(n + 1) * 512], HT[:, ic * 128:(ic + 1) * 128],
                   W1TF[:, ic * 1024 + n * 512: ic * 1024 + (n + 1) * 512],
                   start=(ic == 0), stop=(ic == 15), rd=["ht", "w1tf"],
                   wr=["psB"])
        act(TRASH[:, 0:1024], psB[:], AF.Square, ["psB"], ["trash", "rc"],
            accum=RC[:, 4:5])
        vop(lambda v: v.tensor_scalar(out=RC[:, 6:7], in0=SP[:, 0:1],
                                      scalar1=SP[:, 0:1], scalar2=1.0 / D,
                                      op0=ALU.mult, op1=ALU.mult), ["sp"], ["rc"])
        vop(lambda v: v.tensor_scalar(out=RC[:, 5:6], in0=RC[:, 4:5],
                                      scalar1=RC[:, 6:7], scalar2=EPS,
                                      op0=ALU.mult, op1=ALU.add), ["rc"], ["rc"])
        act(RC[:, 5:6], RC[:, 5:6], AF.Ln, ["rc"], ["rc"])
        act(RC[:, 0:1], RC[:, 5:6], AF.Exp, ["rc"], ["rc"], scale=-0.5)
        vop(lambda v: v.tensor_mul(out=T0[:], in0=psB[:], in1=LNMF[:]),
            ["psB", "lnmf"], ["t0"])
        vop(lambda v: v.tensor_scalar(out=RC[:, 6:7], in0=RC[:, 0:1],
                                      scalar1=SP[:, 0:1], scalar2=None,
                                      op0=ALU.mult), ["rc", "sp"], ["rc"])
        vop(lambda v, qt=qt: v.scalar_tensor_tensor(out=ATT_C, in0=T0[:],
                                                    scalar=RC[:, 6:7], in1=qt,
                                                    op0=ALU.mult, op1=ALU.add),
            ["t0", "rc", qn], ["att", "bigf"])
        vop(lambda v: v.tensor_add(out=X1_CC, in0=XL_C, in1=ATT_C),
            [xn, "att", "bigf"], ["z1", "bigf"])
        layernorm_tile(X1_CC, "z1", "x1t", extran=["bigf"])
        dma("scalar", x1_sc[li * 128:(li + 1) * 128, :], X1_CC, ["x1t", "bigf"], ["x1d"], stream="x1d", no_waw=True)

    if STOP_AFTER is None or STOP_AFTER == "C":
        for li in range(NTILE_L):
            phaseC_tile(li)
    if STOP_AFTER == "C":
        stop_here()

    if tapt("x1", [TLOC, D]) is not None:
        dma("scalar", tap_out["x1"], x1_sc, ["x1d"], ["tap_x1"], stream="tap", no_waw=True)
    if tapt("finstate", [2, 128, 8 * INTER], BF16) is not None:
        dma("scalar", tap_out["finstate"][0], W0T[:], W0N, ["tap_fs"], stream="tap", no_waw=True)
        dma("scalar", tap_out["finstate"][1], W1N[:], W1NN, ["tap_fs"], stream="tap", no_waw=True)
    if tapt("finsc", [1, 8]) is not None:
        dma("scalar", tap_out["finsc"], SR[0:1, :], ["sr"], ["tap_fsc"])
    if tapt("finln", [1, D]) is not None:
        dma("scalar", tap_out["finln"], LNMF[0:1, :], ["lnmf"], ["tap_fln"])

    sch.fence()
    XT0 = W0T[:].bitcast(F32)
    XT1 = W1N[:].bitcast(F32)

    def xt_ap(t):
        return (XT0 if t < 8 else XT1)[:, (t % 8) * 1024:(t % 8 + 1) * 1024]

    def xt_nm(t):
        return f"xt{t}"

    W1St = S1[:]
    W2St = W1TW[:]

    def ffn_layer(l):
        if l == 0:
            vop(lambda v: v.memset(SCT[:, 1026:1032], 0.0), [], ["ht"])
            for t in range(NTILE_L):
                dma("sync", xt_ap(t), x1_sc[t * 128:(t + 1) * 128, :], ["x1d"],
                    [xt_nm(t)], stream="ldC", no_waw=True)
        for ch in range(8):
            dma("sync", ROUT[:, ch * 16:(ch + 1) * 16],
                routers_d[l, ch * 128:(ch + 1) * 128, :], [], ["rout"],
                stream="ldC", no_waw=True)
        vop(lambda v: v.memset(RCNT[:], 0.0), [], ["rcnt"])
        for t in range(NTILE_L):
            for h in range(2):
                for b2 in range(4):
                    bb2 = h * 4 + b2
                    transp(psCf[:, b2 * 128:(b2 + 1) * 128],
                           xt_ap(t)[:, bb2 * 128:(bb2 + 1) * 128], IDENTF[:],
                           "identf", rd=[xt_nm(t)], wr=["psC"])
                vop(lambda v, h=h: v.tensor_copy(
                    out=XTF[:, h * 512:(h + 1) * 512],
                    in_=psCf[:, 0:512]), ["psC"], ["bigf"])
            for ch in range(8):
                mm(psB[:, 0:16], XTF[:, ch * 128:(ch + 1) * 128],
                   ROUT[:, ch * 16:(ch + 1) * 16], start=(ch == 0),
                   stop=(ch == 7), rd=["bigf", "rout"], wr=["psB"])
            vop(lambda v: v.max(out=MX8[:], in_=psB[:, 0:16]), ["psB"], ["mx8"])
            vop(lambda v: v.max_index(out=IXU[:], in_max=MX8[:],
                                      in_values=psB[:, 0:16]), ["mx8", "psB"],
                ["ixu"])
            vop(lambda v: v.tensor_copy(out=IDXF[:], in_=IXU[:, 0:1]), ["ixu"],
                ["idxf"])
            vop(lambda v: v.tensor_scalar(out=RC[:, 7:8], in0=MX8[:, 0:1],
                                          scalar1=-1.0, scalar2=None,
                                          op0=ALU.mult), ["mx8"], ["rc"])
            act(EXW[:], psB[:, 0:16], AF.Exp, ["psB", "rc"], ["exw"],
                bias=RC[:, 7:8])
            vop(lambda v: v.reduce_sum(out=RC[:, 6:7], in_=EXW[:], axis=AX.X),
                ["exw"], ["rc"])
            vop(lambda v: v.reciprocal(out=RC[:, 5:6], in_=RC[:, 6:7]), ["rc"],
                ["rc"])
            vop(lambda v: v.tensor_tensor(out=ISF[:], in0=IOTA[:],
                                          in1=IDXF[:].broadcast_to((128, 16)),
                                          op=ALU.is_equal), ["idxf", "iota"],
                ["isf"])
            vop(lambda v: v.tensor_copy(out=ISB[:], in_=ISF[:]), ["isf"], ["isb"])
            mm(psB[:, 16:32], TRIB[:], ISB[:], True, True, rd=["isb", "trib"],
               wr=["psB"])
            mm(psB[0:1, 32:48], ONESB[:], ISB[:], True, True, rd=["isb", "onesb"],
               wr=["psB"])
            mm(psB[:, 48:64], ONESK1[:], RCNT[:], True, True,
               rd=["rcnt", "onesk1"], wr=["psB"])
            vop(lambda v: v.tensor_copy(out=POSW[:], in_=psB[:, 16:32]), ["psB"],
                ["posw"])
            vop(lambda v: v.tensor_tensor(out=POSW[:], in0=POSW[:],
                                          in1=psB[:, 48:64], op=ALU.add),
                ["posw", "psB"], ["posw"])
            vop(lambda v: v.tensor_tensor(out=RCNT[:], in0=RCNT[:],
                                          in1=psB[0:1, 32:48], op=ALU.add),
                ["rcnt", "psB"], ["rcnt"])
            vop(lambda v: v.tensor_mul(out=ISF[:], in0=POSW[:], in1=ISF[:]),
                ["posw", "isf"], ["isf"])
            vop(lambda v: v.reduce_sum(out=SLOTF[:], in_=ISF[:], axis=AX.X),
                ["isf"], ["slotf"])
            vop(lambda v: v.tensor_scalar(out=RC[:, 7:8], in0=SLOTF[:],
                                          scalar1=float(CAP), scalar2=None,
                                          op0=ALU.is_ge), ["slotf"], ["rc"])
            vop(lambda v: v.scalar_tensor_tensor(out=SLOTF[:], in0=IDXF[:],
                                                 scalar=float(CAP), in1=SLOTF[:],
                                                 op0=ALU.mult, op1=ALU.add),
                ["idxf", "slotf"], ["slotf"])
            vop(lambda v: v.tensor_scalar(out=RC[:, 6:7], in0=RC[:, 7:8],
                                          scalar1=-1.0, scalar2=1.0,
                                          op0=ALU.mult, op1=ALU.add), ["rc"],
                ["rc"])
            vop(lambda v: v.tensor_mul(out=SLOTF[:], in0=SLOTF[:],
                                       in1=RC[:, 6:7]), ["slotf", "rc"],
                ["slotf"])
            vop(lambda v: v.scalar_tensor_tensor(out=SLOTF[:], in0=RC[:, 7:8],
                                                 scalar=float(DUMP),
                                                 in1=SLOTF[:], op0=ALU.mult,
                                                 op1=ALU.add),
                ["rc", "slotf"], ["slotf"])
            vop(lambda v, t=t: v.tensor_copy(out=SLOT_ALL[:, t:t + 1],
                                             in_=SLOTF[:]), ["slotf"],
                ["slotall"])
            vop(lambda v, t=t: v.tensor_copy(out=SCT[:, 0:1024], in_=xt_ap(t)),
                [xt_nm(t)], ["ht"])
            vop(lambda v: v.tensor_copy(out=SCT[:, 1024:1025],
                                        in_=RC[:, 5:6]), ["rc"], ["ht"])
            vop(lambda v: v.tensor_tensor(out=SCT[:, 1025:1026],
                                          in0=RC[:, 5:6],
                                          in1=SCT[:, 1024:1025],
                                          op=ALU.subtract), ["rc", "ht"], ["ht"])
            vop(lambda v: v.tensor_copy(out=SLOTI[:, 0:1], in_=SLOTF[:]),
                ["slotf"], ["sloti"])
            sch.op("gpsimd",
                   lambda g: g.indirect_dma_start(
                       out=eb_d, out_offset=IndirectOffsetOnAxis(SLOTI[:, 0:1], 0),
                       in_=SCT, in_offset=None),
                   reads=["ht", "sloti"], writes=["eb"], inc=16, stream="ebs",
                   no_waw=True)
        for e in range(NEXP):
            for st in range(2):
                dma("sync", GBv,
                    eb_d[e * CAP + st * 128: e * CAP + (st + 1) * 128, :],
                    ["eb", "bigf"], ["bigf"], stream="gbg")
                vop(lambda v, st=st: v.tensor_copy(
                    out=SELG[:, st:st + 1],
                    in_=GBv[:, 1024:1025]), ["bigf"], ["selg"])
                vop(lambda v, st=st: v.tensor_tensor(
                    out=SELG[:, st:st + 1], in0=SELG[:, st:st + 1],
                    in1=GBv[:, 1025:1026], op=ALU.add), ["bigf", "selg"],
                    ["selg"])
                for bq in range(8):
                    transp(psC[:, bq * 128:(bq + 1) * 128],
                           GBv[:, bq * 128:(bq + 1) * 128],
                           IDENTB[:], "identb", rd=["bigf"], wr=["psC"])
                vop(lambda v, st=st: v.tensor_copy(
                    out=XTGv[:, st * 1024:(st + 1) * 1024], in_=psC[:, 0:1024]),
                    ["psC"], ["xtg", "bigf"])
            for hh in range(2):
                for half, qeng, snm in ((0, "sync", "w1stA"),
                                        (1, "gpsimd", "w1stB")):
                    dma(qeng,
                        W1St[:, half * 8192: half * 8192 + 8192]
                        .rearrange("p (c f) -> p c f", c=4),
                        W1b_d[e, half * 512:(half + 1) * 512,
                              hh * 2048:(hh + 1) * 2048]
                        .rearrange("(c p) f -> p c f", p=128),
                        [snm], [snm], stream=snm,
                        waits=sch.barrier_of([f"wc1_{e}"]))
                for st in range(2):
                    for kk in range(8):
                        for n in range(4):
                            mm(psA[:, n * 512:(n + 1) * 512],
                               XTGv[:, st * 1024 + kk * 128:
                                    st * 1024 + (kk + 1) * 128],
                               W1St[:, kk * 2048 + n * 512:
                                    kk * 2048 + (n + 1) * 512],
                               start=(kk == 0), stop=(kk == 7),
                               rd=["xtg", "w1stA" if kk < 4 else "w1stB"],
                               wr=["psA"])
                    sigtriple(DA[:], "da", psA[:], "psA", -1.0, GSBf, "gsb0",
                              TRASH[:], "trash")
                    vop(lambda v, st=st, hh=hh: v.scalar_tensor_tensor(
                        out=HFv[:, st * FF + hh * 2048:
                                st * FF + (hh + 1) * 2048],
                        in0=psA[:], scalar=1.0, in1=DA[:], op0=ALU.mult,
                        op1=ALU.mult), ["psA", "da"], [f"hf{st}", "S0"])
            for st in range(2):
                for g in range(2):
                    for bq in range(16):
                        transp(psC[:, bq * 128:(bq + 1) * 128],
                               HFv[:, st * FF + g * 2048 + bq * 128:
                                   st * FF + g * 2048 + (bq + 1) * 128],
                               IDENTB[:], "identb", rd=[f"hf{st}"], wr=["psC"])
                    vop(lambda v, st=st, g=g: v.tensor_copy(
                        out=HFv[:, st * FF + g * 2048: st * FF + (g + 1) * 2048],
                        in_=psC[:]), ["psC"], [f"hf{st}", "S0"])
            psY = {0: psB[:, 0:1024], 1: psA[:, 0:1024]}
            psYn = {0: "psB", 1: "psA"}
            for q in range(4):
                for half, qeng, snm in ((0, "sync", "w2sA"),
                                        (1, "gpsimd", "w2sB")):
                    dma(qeng,
                        W2St[:, half * 4096: half * 4096 + 4096]
                        .rearrange("p (c f) -> p c f", c=4),
                        W2b_d[e, q * 1024 + half * 512:
                              q * 1024 + (half + 1) * 512, :]
                        .rearrange("(c p) f -> p c f", p=128),
                        (["w1tw%d" % i for i in range(8)]
                         if (l == 0 and e == 0 and q == 0) else [snm]),
                        [snm], stream=snm,
                        waits=sch.barrier_of([f"wc2_{e}"]))
                for st in range(2):
                    for kk in range(8):
                        ic = q * 8 + kk
                        for n in range(2):
                            mm(psY[st][:, n * 512:(n + 1) * 512],
                               HFv[:, st * FF + ic * 128:
                                   st * FF + (ic + 1) * 128],
                               W2St[:, kk * 1024 + n * 512:
                                    kk * 1024 + (n + 1) * 512],
                               start=(ic == 0), stop=(ic == 31),
                               rd=[f"hf{st}", "w2sA" if kk < 4 else "w2sB"],
                               wr=[psYn[st]])
            for st in range(2):
                act(OGv[:, st * 1024:(st + 1) * 1024], psY[st], AF.Copy,
                    [psYn[st], "selg"], [f"og{st}", "S0"],
                    scale=SELG[:, st:st + 1])
                dma("scalar",
                    ebo_d[e * CAP + st * 128: e * CAP + (st + 1) * 128, :],
                    OGv[:, st * 1024:(st + 1) * 1024], [f"og{st}"], ["ebod"],
                    stream="ebod", no_waw=True)
        for t in range(NTILE_L):
            vop(lambda v, t=t: v.tensor_copy(out=SLOTI[:, 1:2],
                                             in_=SLOT_ALL[:, t:t + 1]),
                ["slotall"], ["sloti2"])
            if l < NLAY - 1:
                sch.op("gpsimd",
                       lambda g, t=t: g.indirect_dma_start(
                           out=xt_ap(t), out_offset=None,
                           in_=ebo_d,
                           in_offset=IndirectOffsetOnAxis(SLOTI[:, 1:2], 0)),
                       reads=["sloti2", "ebod"], writes=[xt_nm(t)], inc=16,
                       stream="ebg", no_waw=True)
            else:
                sch.op("gpsimd",
                       lambda g: g.indirect_dma_start(
                           out=OGv[:, 0:1024], out_offset=None,
                           in_=ebo_d,
                           in_offset=IndirectOffsetOnAxis(SLOTI[:, 1:2], 0)),
                       reads=["sloti2", "ebod"], writes=["og0", "S0"], inc=16,
                       stream="ebg")
                dma("sync", X1T_F, x1_sc[t * 128:(t + 1) * 128, :], ["x1d"],
                    ["x1tf", "S0"], stream="ldxl")
                vop(lambda v: v.tensor_add(out=X1_CF, in0=X1T_F,
                                           in1=OGv[:, 0:1024]),
                    ["x1tf", "og0"], ["z1", "S0"])
                layernorm_tile(X1_CF, "z1", "x2t", extran=["S0"])
                dma("sync", out_x2[t * 128:(t + 1) * 128, :], X1_CF,
                    ["x2t", "S0"], ["outd"], stream="outd", no_waw=True)
        if f"ffn{l}" in taps and l < NLAY - 1:
            tw = tapt(f"ffn{l}", [TLOC, D])
            for t in range(NTILE_L):
                dma("scalar", tw[t * 128:(t + 1) * 128, :], xt_ap(t),
                    [xt_nm(t)], [f"tap_ffn{l}"], stream="tap", no_waw=True)

    if STOP_AFTER is None:
        for l in range(min(NLAY, FFNPART)):
            ffn_layer(l)

    if STOP_AFTER is None:
        if FFNPART < NLAY:
            stop_here()
        else:
            tapnames = [nm for nm in sch.bufs if nm.startswith("tap_")]
            sch.final_waits = sch.barrier_of(["outd"] + tapnames)

    with ExitStack() as semstack:
        sems = {k: semstack.enter_context(nc.semaphore("sem_" + str(i)))
                for i, k in enumerate(sch.sem_keys)}
        with nc.Block() as block:
            def mkbody(en):
                def run(eng):
                    for (waits, fn, key, inc) in sch.ops[en]:
                        for pk, pv in waits.items():
                            eng.wait_ge(sems[pk], pv)
                        inst = fn(eng)
                        inst.then_inc(sems[key], inc)
                    if en == "sync":
                        for pk, pv in sch.final_waits:
                            eng.wait_ge(sems[pk], pv)
                return run
            block.tensor(mkbody("tensor"))
            block.vector(mkbody("vector"))
            block.scalar(mkbody("scalar"))
            block.gpsimd(mkbody("gpsimd"))
            block.sync(mkbody("sync"))
        stack.close()
    return nc, tap_out


_cache = {}


def _get_program(taps=()):
    key = tuple(sorted(taps))
    if key not in _cache:
        _cache[key] = build_program(taps)
    return _cache[key]


def _consts():
    c = {}
    c["identb"] = np.eye(128, dtype=ml_dtypes.bfloat16)
    c["identf"] = np.eye(128, dtype=np.float32)
    c["trib"] = np.triu(np.ones((128, 128)), 1).astype(ml_dtypes.bfloat16)
    c["iota16"] = np.arange(16, dtype=np.float32).reshape(1, 16)
    c["ones128b"] = np.ones((128, 1), dtype=ml_dtypes.bfloat16)
    c["onesK1f"] = np.ones((1, 128), dtype=np.float32)
    h = np.zeros((128, 2), dtype=np.float32)
    h[0:64, 0] = 1.0
    h[64:128, 1] = 1.0
    c["halfones"] = h
    c["zeb"] = np.zeros((EBROWS, 1032), dtype=ml_dtypes.bfloat16)
    c["zebo"] = np.zeros((EBROWS, 1024), dtype=np.float32)
    return c


def make_inmaps(inputs):
    x = np.ascontiguousarray(np.asarray(inputs["L0_attn_x"], dtype=np.float32)
                             .reshape(TOK, D))
    base = dict(
        x_full=x,
        wkT=np.ascontiguousarray(np.asarray(inputs["L0_attn_wk_w"]).T),
        wvT=np.ascontiguousarray(np.asarray(inputs["L0_attn_wv_w"]).T),
        wqT=np.ascontiguousarray(np.asarray(inputs["L0_attn_wq_w"]).T),
        knorm=np.ascontiguousarray(np.asarray(inputs["L0_attn_k_norm_w"])
                                   .reshape(1, D)),
        qnorm=np.ascontiguousarray(np.asarray(inputs["L0_attn_q_norm_w"])
                                   .reshape(1, D)),
        threp=np.ascontiguousarray(np.tile(
            np.asarray(inputs["L0_attn_theta_w"]).reshape(CHUNK, D), (2, 1))),
        alrep=np.ascontiguousarray(np.tile(
            np.asarray(inputs["L0_attn_alpha_w"]).reshape(CHUNK, D), (2, 1))),
        etrep=np.ascontiguousarray(np.tile(
            np.asarray(inputs["L0_attn_eta_w"]).reshape(CHUNK, D), (2, 1))),
        w0T=np.ascontiguousarray(np.asarray(inputs["L0_attn_mem_w0"]).T),
        w1n=np.ascontiguousarray(np.asarray(inputs["L0_attn_mem_w1"])),
        lnw=np.ascontiguousarray(np.asarray(inputs["L0_attn_mem_ln_w"])
                                 .reshape(1, D)),
        routers=np.ascontiguousarray(np.asarray(inputs["L0_ffn_routers"])),
        W1f=np.ascontiguousarray(np.asarray(inputs["L0_ffn_W1"])),
        W2f=np.ascontiguousarray(np.asarray(inputs["L0_ffn_W2"])),
        **_consts(),
    )
    in_maps = []
    for ci in range(NCORE):
        m = dict(base)
        m["x_loc"] = np.ascontiguousarray(x[ci * TLOC:(ci + 1) * TLOC])
        in_maps.append(m)
    return in_maps


def kernel(**inputs):
    nc, _ = _get_program()
    in_maps = make_inmaps(inputs)
    res = run_bass_kernel_spmd(nc, in_maps, core_ids=list(range(NCORE)))
    out = np.concatenate([np.asarray(r["out_x2"]) for r in res.results], axis=0)
    return out.reshape(B, S, D).astype(np.float32)


# version 07 (final)
# speedup vs baseline: 1.1602x; 1.1602x over previous
import numpy as np
import ml_dtypes
import concourse.bass as bass
import concourse.mybir as mybir
from concourse.bass import IndirectOffsetOnAxis
from concourse.bass_utils import run_bass_kernel_spmd
from contextlib import ExitStack

F32 = mybir.dt.float32
BF16 = mybir.dt.bfloat16
I32 = mybir.dt.int32
U32 = mybir.dt.uint32
AF = mybir.ActivationFunctionType
ALU = mybir.AluOpType
AX = mybir.AxisListType

NCORE = 8
B, S, D = 2, 8192, 1024
TOK = B * S
TLOC = TOK // NCORE
NTILE_G = TOK // 128
NTILE_L = TLOC // 128
NTB = NTILE_G // 2
CHUNK = 64
NC = 2 * NTB
INTER = 2048
FF = 4096
NEXP = 16
NLAY = 4
CAP = 256
EBROWS = NEXP * CAP + 8
DUMP = NEXP * CAP
EPS = 1e-6
LN_EPS = 1e-5
KAPPA = 2.0 / (128 * 1024)
BASE_LR = 0.01

ENGS = ("tensor", "vector", "scalar", "gpsimd", "sync")
STOP_AFTER = None
SCANPART = 99
GTPART = 9
FFNPART = 99


PSUM_BUFS = ("psA", "psB", "psC")


class _Buf:
    __slots__ = ("writer", "readers", "last_read")

    def __init__(self):
        self.writer = None
        self.readers = []
        self.last_read = {}


class Sched:

    def __init__(self):
        self.ops = {e: [] for e in ENGS}
        self.cum = {}
        self.mark = {}
        self.bufs = {}
        self.pending_fence = None
        self.fenced = set()
        self.final_waits = []
        self.sem_keys = []

    def _key(self, k):
        if k not in self.cum:
            self.cum[k] = 0
            self.sem_keys.append(k)
        return k

    def buf(self, name):
        b = self.bufs.get(name)
        if b is None:
            b = self.bufs[name] = _Buf()
        return b

    def fence(self):
        self.pending_fence = dict(self.cum)
        self.fenced = set()

    def op(self, eng, fn, reads=(), writes=(), inc=1, waits=(), stream=None,
           no_waw=False):
        key = self._key(stream if stream is not None else eng)
        need = {}

        def add(pe, pk, pv):
            if pe == eng and eng == "tensor" and pk == "tensor":
                return
            if pk not in ENGS:
                pv = self.cum[pk]
            need[pk] = max(need.get(pk, 0), pv)

        if inc == 16:
            prev = self.cum[key]
            if prev > 0:
                need[key] = max(need.get(key, 0), prev)
        if self.pending_fence is not None and eng not in self.fenced:
            for pk, pv in self.pending_fence.items():
                if pv > 0 and not (pk == eng == "tensor"):
                    need[pk] = max(need.get(pk, 0), pv)
            self.fenced.add(eng)
        for nm in reads:
            b = self.buf(nm)
            if b.writer:
                add(*b.writer)
            if nm.startswith("ps") and eng in ("scalar", "vector"):
                other = "vector" if eng == "scalar" else "scalar"
                lr = b.last_read.get(other)
                if lr:
                    add(*lr)
        for nm in writes:
            b = self.buf(nm)
            if b.writer and not no_waw:
                add(*b.writer)
            for r in b.readers:
                add(*r)
        for pk, pv in waits:
            need[pk] = max(need.get(pk, 0), pv)
        final = {}
        for pk, pv in need.items():
            if pk == key and inc == 1 and pv >= self.cum[key] + inc:
                continue
            if self.mark.get((eng, pk), 0) >= pv:
                continue
            final[pk] = pv
            self.mark[(eng, pk)] = pv
        self.cum[key] += inc
        mycum = self.cum[key]
        self.ops[eng].append((final, fn, key, inc))
        for nm in reads:
            b = self.buf(nm)
            b.readers.append((eng, key, mycum))
            b.last_read[eng] = (eng, key, mycum)
        for nm in writes:
            b = self.buf(nm)
            b.writer = (eng, key, mycum)
            b.readers = []
            b.last_read = {}
        return (key, mycum)

    def barrier_of(self, names):
        w = {}
        for nm in names:
            b = self.bufs.get(nm)
            if b is not None and b.writer:
                pe, pk, pv = b.writer
                if pk not in ENGS:
                    pv = self.cum[pk]
                w[pk] = max(w.get(pk, 0), pv)
        return list(w.items())


def build_program(taps=()):
    nc = bass.Bass("TRN2", target_bir_lowering=False, debug=False)
    taps = set(taps)

    def din(name, shape, dtype=F32):
        return nc.dram_tensor(name, list(shape), dtype, kind="ExternalInput").ap()

    x_full = din("x_full", [TOK, D])
    x_loc = din("x_loc", [TLOC, D])
    wkT = din("wkT", [D, D])
    wvT = din("wvT", [D, D])
    wqT = din("wqT", [D, D])
    knorm = din("knorm", [1, D])
    qnorm = din("qnorm", [1, D])
    threp_d = din("threp", [128, D])
    alrep_d = din("alrep", [128, D])
    etrep_d = din("etrep", [128, D])
    w0T_d = din("w0T", [D, INTER])
    w1n_d = din("w1n", [D, INTER])
    lnw_d = din("lnw", [1, D])
    routers_d = din("routers", [NLAY, D, NEXP])
    W1f_d = din("W1f", [NEXP, D, FF])
    W2f_d = din("W2f", [NEXP, FF, D])
    identb_d = din("identb", [128, 128], BF16)
    identf_d = din("identf", [128, 128], F32)
    trib_d = din("trib", [128, 128], BF16)
    iota16_d = din("iota16", [1, 16])
    ones128b_d = din("ones128b", [128, 1], BF16)
    onesK1f_d = din("onesK1f", [1, 128])
    halfones_d = din("halfones", [128, 2])
    zeb_d = din("zeb", [EBROWS, 1032], BF16)
    zebo_d = din("zebo", [EBROWS, D])

    out_x2 = nc.dram_tensor("out_x2", [TLOC, D], F32, kind="ExternalOutput").ap()

    def scr(name, shape, dtype):
        return nc.dram_tensor(name, list(shape), dtype, kind="Internal").ap()

    ktok_d = scr("ktok_d", [NC, 128, D], BF16)
    kmv_d = scr("kmv_d", [NC, 128, D], BF16)
    qtok_d = scr("qtok_d", [TLOC, D], BF16)
    zd_d = scr("zd_d", [2, 3 * NTILE_G], F32)
    th_d = scr("th_d", [2, NC], F32)
    etab_d = scr("etab_d", [2, NC], F32)
    w1n_sc = scr("w1n_sc", [D, INTER], BF16)
    ln_sc = scr("ln_sc", [1, D], F32)
    W1b_d = scr("W1b_d", [NEXP, D, FF], BF16)
    W2b_d = scr("W2b_d", [NEXP, FF, D], BF16)
    x1_sc = scr("x1_sc", [TLOC, D], F32)
    eb_d = scr("eb_d", [EBROWS, 1032], BF16)
    ebo_d = scr("ebo_d", [EBROWS, D], F32)

    tap_out = {}

    def tapt(name, shape, dtype=F32):
        if name in taps and name not in tap_out:
            tap_out[name] = nc.dram_tensor("tap_" + name, list(shape), dtype,
                                           kind="ExternalOutput").ap()
        return tap_out.get(name)

    sch = Sched()
    stack = ExitStack()

    def sb(name, shape, dtype):
        return stack.enter_context(nc.sbuf_tensor(name, list(shape), dtype))

    W0T = sb("W0T", [128, 8 * INTER], BF16)
    W1N = sb("W1N", [128, 8 * INTER], BF16)
    S0 = sb("S0", [128, 8 * INTER], BF16)
    S1 = sb("S1", [128, 8 * INTER], BF16)
    W1TW = sb("W1TW", [128, 8 * 1024], BF16)
    BIGF = sb("BIGF", [128, 4096], F32)
    BIGFb = BIGF[:].bitcast(BF16)
    GSB = sb("GSB", [128, 4096], BF16)
    GSBf = GSB[:].bitcast(F32)
    TRASH = sb("TRASH", [128, INTER], BF16)
    TRASHF = TRASH[:].bitcast(F32)
    HTOK = sb("HTOK", [128, INTER], BF16)
    HTOKf = HTOK[:].bitcast(F32)
    HT = sb("HT", [128, INTER], BF16)
    T0 = sb("T0", [128, D], BF16)
    E0 = sb("E0", [128, D], BF16)
    T1 = sb("T1", [128, D], BF16)
    DA = sb("DA", [128, INTER], BF16)
    DY = sb("DY", [128, D], BF16)
    ZLN = T0
    DYT = E0
    KTOK = sb("KTOK", [128, D], BF16)
    KMV = sb("KMV", [128, D], BF16)
    LNMF = sb("LNMF", [128, D], F32)
    SLNR = sb("SLNR", [128, D], BF16)
    PAN = sb("PAN", [128, 3 * NC + 64], F32)
    THCOL = PAN[:, 0:NC]
    ETACOL = PAN[:, NC:2 * NC]
    BETACOL = PAN[:, 2 * NC:3 * NC]
    NORMP = PAN[:, 3 * NC:3 * NC + 34]
    ZSB = PAN[0:2, 0:3 * NTILE_G]
    THsb = PAN[32:34, 0:NC]
    ALsb = PAN[32:34, NC:2 * NC]
    ETsb = PAN[32:34, 2 * NC:3 * NC]
    TH2 = PAN[64:66, 0:NC]
    EB2 = PAN[64:65, NC:3 * NC]
    SP = sb("SP", [128, 8], F32)
    SR = sb("SR", [1, 8], F32)
    RC = sb("RC", [128, 8], F32)
    MX8 = sb("MX8", [128, 8], F32)
    IXU = sb("IXU", [128, 8], U32)
    SELG = sb("SELG", [128, 2], F32)
    EXW = sb("EXW", [128, 16], F32)
    IDXF = sb("IDXF", [128, 1], F32)
    ISB = sb("ISB", [128, 16], BF16)
    ISF = sb("ISF", [128, 16], F32)
    POSW = sb("POSW", [128, 16], F32)
    RCNT = sb("RCNT", [1, 16], F32)
    SLOTF = sb("SLOTF", [128, 1], F32)
    SLOTI = sb("SLOTI", [128, 2], I32)
    SLOT_ALL = sb("SLOT_ALL", [128, 16], I32)
    ROUT = sb("ROUT", [128, 8 * 16], F32)
    IDENTB = sb("IDENTB", [128, 128], BF16)
    IDENTF = sb("IDENTF", [128, 128], F32)
    TRIB = sb("TRIB", [128, 128], BF16)
    IOTA = sb("IOTA", [128, 16], F32)
    ONESB = sb("ONESB", [128, 1], BF16)
    ONESK1 = sb("ONESK1", [1, 128], F32)
    HALFON = sb("HALFON", [128, 2], F32)
    ONESF = sb("ONESF", [128, 1], F32)
    ONESH = sb("ONESH", [128, 1], F32)
    SG = BIGF[:, 0:2048]
    UB = BIGF[:, 2048:4096]
    KT = TRASH[:, 0:1024]
    XT_A = BIGF[:, 0:2048]
    KDST_A = GSB[:, 0:1024]
    XTT_A = HT[:, 0:1024]
    KP_A = BIGF[:, 2048:3072]
    VT_A = DA[:, 0:1024]
    QT_A = DA[:, 1024:2048]
    SGA = HTOKf
    T1A = BIGF[:, 3072:4096]
    WKT = S1[:, 0:8 * 1024]
    WVT = W1TW[:]
    WQT = S1[:, 8 * 1024:16 * 1024]
    S0f = S0[:].bitcast(F32)
    THREP = S0f[:, 0:1024]
    ALREP = S0f[:, 1024:2048]
    ETREP = S0f[:, 2048:3072]
    KNREP = S0f[:, 3072:4096]
    QNREP = S0f[:, 4096:5120]
    ZPAN = S0f[:, 5120:5120 + 3 * NTILE_G]
    QTOK_C = KTOK[:]
    XL_C = BIGF[:, 0:1024]
    ATT_C = BIGF[:, 2048:3072]
    X1_CC = BIGF[:, 3072:4096]
    XTF = BIGF[:, 0:1024]
    GBv = BIGFb[:, 2048:3080]
    XTGv = BIGFb[:, 4112:6160]
    SCT = HT[:, 0:1032]
    HFv = S0[:, 0:8192]
    OGv = S0f[:, 4096:6144]
    X1T_F = S0f[:, 6144:7168]
    X1_CF = S0f[:, 7168:8192]

    psA = stack.enter_context(nc.psum_tensor("psA", [128, 2048], F32))
    psB = stack.enter_context(nc.psum_tensor("psB", [128, 1024], F32))
    psC = stack.enter_context(nc.psum_tensor("psC", [128, 2048], BF16))
    psAb = psA[:].bitcast(BF16)
    psCf = psC[:].bitcast(F32)

    def mm(out_ap, lhsT_ap, rhs_ap, start, stop, rd, wr, waits=()):
        return sch.op("tensor",
                      lambda t, o=out_ap, l=lhsT_ap, r=rhs_ap, s0=start, s1=stop:
                      t.matmul(o, l, r, start=s0, stop=s1),
                      reads=rd, writes=wr, waits=waits)

    def transp(out_ap, in_ap, ident, identn, rd, wr, waits=()):
        return sch.op("tensor",
                      lambda t, o=out_ap, i=in_ap, e=ident: t.transpose(o, i, e),
                      reads=list(rd) + [identn], writes=wr, waits=waits)

    def act(out_ap, in_ap, func, rd, wr, scale=1.0, bias=0.0, accum=None, waits=()):
        return sch.op("scalar",
                      lambda s, o=out_ap, i=in_ap, f=func, sc=scale, b=bias, a=accum:
                      s.activation(out=o, in_=i, func=f, bias=b, scale=sc,
                                   accum_out=a),
                      reads=rd, writes=wr, waits=waits)

    def vop(fn, rd, wr, eng="vector", waits=()):
        return sch.op(eng, fn, reads=rd, writes=wr, waits=waits)

    def dma(eng, out_ap, in_ap, rd, wr, stream="ld0", waits=(), no_waw=False, **kw):
        return sch.op(eng,
                      lambda e, o=out_ap, i=in_ap, k=kw: e.dma_start(out=o, in_=i, **k),
                      reads=rd, writes=wr, inc=16, waits=waits, stream=stream,
                      no_waw=no_waw)

    def _l(x):
        return [x] if isinstance(x, str) else list(x)

    def sigtriple(dst_ap, dstn, src_ap, srcn, scale_neg, t1_ap, t1n, t2_ap, t2n,
                  extra_rd=()):
        act(t1_ap, src_ap, AF.Exp, _l(srcn) + list(extra_rd), _l(t1n),
            scale=scale_neg)
        vop(lambda v, t=t1_ap: v.tensor_scalar(out=t, in0=t, scalar1=1e18,
                                               scalar2=None, op0=ALU.min),
            _l(t1n), _l(t1n))
        act(t2_ap, t1_ap, AF.Ln, _l(t1n), _l(t2n), bias=1.0)
        act(dst_ap, t2_ap, AF.Exp, _l(t2n), _l(dstn), scale=-1.0)

    dma("sync", IDENTB[:], identb_d, [], ["identb"])
    dma("sync", IDENTF[:], identf_d, [], ["identf"])
    dma("sync", TRIB[:], trib_d, [], ["trib"])
    dma("sync", IOTA[:], iota16_d.broadcast_to((128, 16)), [], ["iota"])
    dma("sync", ONESB[:], ones128b_d, [], ["onesb"])
    dma("sync", ONESK1[:], onesK1f_d, [], ["onesk1"])
    dma("sync", HALFON[:], halfones_d, [], ["halfones"])
    vop(lambda v: v.memset(ONESF[:], 1.0), [], ["onesf"])
    vop(lambda v: v.memset(ONESH[:], 0.5), [], ["ones2"])
    dma("sync", THREP, threp_d, [], ["s0"], no_waw=True)
    dma("sync", ALREP, alrep_d, [], ["s0"], no_waw=True)
    dma("sync", ETREP, etrep_d, [], ["s0"], no_waw=True)
    dma("sync", KNREP, knorm.broadcast_to((128, D)), [], ["s0"], no_waw=True)
    dma("sync", QNREP, qnorm.broadcast_to((128, D)), [], ["s0"], no_waw=True)
    for c in range(8):
        dma("gpsimd", WKT[:, c * 1024:(c + 1) * 1024], wkT[c * 128:(c + 1) * 128, :],
            [], ["s1"], stream="ldg", no_waw=True)
        dma("gpsimd", WVT[:, c * 1024:(c + 1) * 1024], wvT[c * 128:(c + 1) * 128, :],
            [], ["w1tw"], stream="ldg", no_waw=True)
        dma("gpsimd", WQT[:, c * 1024:(c + 1) * 1024], wqT[c * 128:(c + 1) * 128, :],
            [], ["s1"], stream="ldg", no_waw=True)
    vop(lambda v: v.memset(ZPAN, 0.0), ["s0"], ["zpan"])

    def transpose_tile(dst_ap, dstn, src_ap, src_names, identap, identn, f32):
        pc = psCf if f32 else psC
        for h in range(2):
            for c4 in range(4):
                cc = h * 4 + c4
                transp(pc[:, c4 * 128:(c4 + 1) * 128],
                       src_ap[:, cc * 128:(cc + 1) * 128], identap, identn,
                       rd=src_names, wr=["psC"])
            vop(lambda v, h=h: v.tensor_copy(out=dst_ap[:, h * 512:(h + 1) * 512],
                                             in_=pc[:, 0:512]), ["psC"], [dstn])

    def projection(dst_name, dst_ap, wt_ap, wt_name, nrm_rep, do_norm,
                   xtt, xtt_n, half):
        panames = [f"psA{2 * half}", f"psA{2 * half + 1}"]
        for c in range(8):
            for n in range(2):
                mm(psA[:, half * 1024 + n * 512: half * 1024 + (n + 1) * 512],
                   xtt[:, c * 128:(c + 1) * 128],
                   wt_ap[:, c * 1024 + n * 512: c * 1024 + (n + 1) * 512],
                   start=(c == 0), stop=(c == 7),
                   rd=[wt_name, xtt_n], wr=[panames[n]])
        pa = psA[:, half * 1024: half * 1024 + 1024]
        sigtriple(SGA, "sga", pa, panames, -1.0, T1A, "bigf", TRASH[:, 0:1024],
                  "trash")
        if not do_norm:
            vop(lambda v: v.scalar_tensor_tensor(out=dst_ap, in0=pa, scalar=1.0,
                                                 in1=SGA, op0=ALU.mult,
                                                 op1=ALU.mult),
                panames + ["sga"], [dst_name])
        else:
            vop(lambda v: v.scalar_tensor_tensor(out=KP_A, in0=pa, scalar=1.0,
                                                 in1=SGA, op0=ALU.mult,
                                                 op1=ALU.mult),
                panames + ["sga"], ["bigf"])
            act(TRASH[:, 0:1024], KP_A, AF.Square, ["bigf"], ["trash", "rc"],
                accum=RC[:, 4:5])
            vop(lambda v: v.tensor_scalar(out=RC[:, 5:6], in0=RC[:, 4:5],
                                          scalar1=1.0 / D, scalar2=EPS,
                                          op0=ALU.mult, op1=ALU.add), ["rc"],
                ["rc"])
            act(RC[:, 5:6], RC[:, 5:6], AF.Ln, ["rc"], ["rc"])
            act(RC[:, 5:6], RC[:, 5:6], AF.Exp, ["rc"], ["rc"], scale=-0.5)
            vop(lambda v: v.scalar_tensor_tensor(out=dst_ap, in0=KP_A,
                                                 scalar=RC[:, 5:6], in1=nrm_rep,
                                                 op0=ALU.mult, op1=ALU.mult),
                ["bigf", "rc", "s0"], [dst_name])

    def phaseA_tile(ti, qonly, src, li=None):
        par = ti % 2
        xt = XT_A[:, par * 1024:(par + 1) * 1024]
        xname = f"xA{par}"
        dma("sync", xt, src, [], [xname, "bigf"], stream="ldx")
        xtt = HT[:, par * 1024:(par + 1) * 1024]
        xtn = f"ht{par}"
        transpose_tile(xtt, xtn, xt, [xname, "bigf"], IDENTF[:], "identf", True)
        if not qonly:
            bb = ti // NTB
            cpair = 2 * (ti % NTB)
            projection("ktA", KDST_A, WKT, "s1", KNREP, True, xtt, xtn, 0)
            projection("vtA", VT_A, WVT, "w1tw", None, False, xtt, xtn, 1)
            vop(lambda v: v.tensor_sub(out=VT_A, in0=KDST_A, in1=VT_A),
                ["ktA", "vtA"], ["vtA", "da"])
            for half in range(2):
                rows = slice(half * 64, half * 64 + 64)
                cch = cpair + half
                dma("scalar", ktok_d[cch, bb * 64:(bb + 1) * 64, :],
                    KDST_A[rows, :], ["ktA"], ["ktokd"], stream="stK",
                    no_waw=True)
                dma("scalar", kmv_d[cch, bb * 64:(bb + 1) * 64, :], VT_A[rows, :],
                    ["vtA"], ["kmvd"], stream="stM", no_waw=True)
            for wi, wrep in enumerate((THREP, ALREP, ETREP)):
                vop(lambda v, wrep=wrep, wi=wi, xt=xt, ti=ti:
                    v.scalar_tensor_tensor(
                        out=TRASHF, in0=xt, scalar=1.0, in1=wrep, op0=ALU.mult,
                        op1=ALU.mult,
                        accum_out=ZPAN[:, wi * NTILE_G + ti: wi * NTILE_G + ti + 1]),
                    [xname, "s0", "zpan", "bigf"], ["zpan", "trash"])
        else:
            projection("qtA", QT_A, WQT, "s1", QNREP, True, xtt, xtn, 0)
            dma("scalar", qtok_d[li * 128:(li + 1) * 128, :], QT_A,
                ["qtA", "da"], ["qtokd"], stream="stQ", no_waw=True)

    for ti in range(NTILE_G):
        phaseA_tile(ti, False, x_full[ti * 128:(ti + 1) * 128, :])
    for li in range(NTILE_L):
        phaseA_tile(li, True, x_loc[li * 128:(li + 1) * 128, :], li=li)

    mm(psB[0:2, 0:3 * NTILE_G], HALFON[:], ZPAN, True, True, ["zpan", "halfones"], ["psB"])
    vop(lambda v: v.tensor_copy(out=ZSB, in_=psB[0:2, 0:3 * NTILE_G]), ["psB"], ["pan"])
    dma("scalar", zd_d, ZSB, ["pan"], ["zdd"], stream="stA")
    for wi, dst in enumerate((THsb, ALsb, ETsb)):
        src = zd_d[:, wi * NTILE_G:(wi + 1) * NTILE_G]
        src3 = src.rearrange("h (b cc) -> b cc h", b=2)
        dma("scalar", dst.rearrange("b (cc h) -> b cc h", cc=NTB), src3,
            ["zdd"], ["pan"], stream="ldpanS", no_waw=True,
            allow_slow_non_contiguous=True)
    sigtriple(TH2, "pan", THsb, "pan", -1.0, BIGF[0:2, 0:NC], "bigf",
              TRASH[0:2, 0:NC], "trash")
    vop(lambda v: v.tensor_scalar(out=TH2, in0=TH2, scalar1=BASE_LR,
                                  scalar2=None, op0=ALU.mult), ["pan"], ["pan"])
    dma("scalar", th_d, TH2, ["pan"], ["thd"], stream="stA")
    mm(psB[0:1, 0:NC], ONESH[32:34, :], ETsb, True, True, ["pan", "ones2"], ["psB"])
    mm(psB[0:1, NC:2 * NC], ONESH[32:34, :], ALsb, True, True, ["pan", "ones2"], ["psB"])
    vop(lambda v: v.tensor_copy(out=EB2, in_=psB[0:1, 0:2 * NC]), ["psB"], ["pan"])
    sigtriple(EB2, "pan", EB2, "pan", -1.0, BIGF[0:1, 0:2 * NC], "bigf",
              TRASH[0:1, 0:2 * NC], "trash")
    vop(lambda v: v.tensor_scalar(out=EB2[0:1, NC:2 * NC],
                                  in0=EB2[0:1, NC:2 * NC], scalar1=-1.0,
                                  scalar2=1.0, op0=ALU.mult, op1=ALU.add),
        ["pan"], ["pan"])
    dma("scalar", etab_d[0:1, :], EB2[0:1, 0:NC], ["pan"], ["etabd"], stream="stA", no_waw=True)
    dma("scalar", etab_d[1:2, :], EB2[0:1, NC:2 * NC], ["pan"], ["etabd"], stream="stA", no_waw=True)
    dma("sync", THCOL[0:64, :], th_d[0:1, :].broadcast_to((64, NC)), ["thd"],
        ["pan"], stream="ldpan", no_waw=True)
    dma("sync", THCOL[64:128, :], th_d[1:2, :].broadcast_to((64, NC)), ["thd"],
        ["pan"], stream="ldpan", no_waw=True)
    dma("sync", ETACOL, etab_d[0:1, :].broadcast_to((128, NC)), ["etabd"], ["pan"], stream="ldpan", no_waw=True)
    dma("sync", BETACOL, etab_d[1:2, :].broadcast_to((128, NC)), ["etabd"], ["pan"], stream="ldpan", no_waw=True)

    if tapt("theta", [2, NC]) is not None:
        dma("scalar", tap_out["theta"], th_d, ["thd"], ["tap_theta"], stream="tap", no_waw=True)
    if tapt("etab", [2, NC]) is not None:
        dma("scalar", tap_out["etab"], etab_d, ["etabd"], ["tap_etab"], stream="tap", no_waw=True)
    if tapt("ktok", [NC, 128, D], BF16) is not None:
        dma("scalar", tap_out["ktok"], ktok_d, ["ktokd"], ["tap_ktok"], stream="tap", no_waw=True)
    if tapt("kmv", [NC, 128, D], BF16) is not None:
        dma("scalar", tap_out["kmv"], kmv_d, ["kmvd"], ["tap_kmv"], stream="tap", no_waw=True)
    if tapt("qtok", [TLOC, D], BF16) is not None:
        dma("scalar", tap_out["qtok"], qtok_d, ["qtokd"], ["tap_qtok"], stream="tap", no_waw=True)

    def stop_here():
        sch.final_waits = [(k, v) for k, v in sch.cum.items() if v > 0]

    if STOP_AFTER == "A":
        stop_here()
    W0N = [f"W0T_{i}" for i in range(8)]
    W1NN = [f"W1N_{i}" for i in range(8)]
    S0N = [f"S0_{i}" for i in range(8)]
    S1N = [f"S1_{i}" for i in range(8)]
    PSAF = ["psA0", "psA1", "psA2", "psA3"]
    PSBF = ["psB0", "psB1"]
    KTOK2 = GSB[:, 2048:3072]
    KMV2 = GSB[:, 3072:4096]
    GDUMP = GSB[:, 0:2048]
    sch.fence()
    for c in range(8):
        dma("gpsimd", W0T[:, c * INTER:(c + 1) * INTER],
            w0T_d[c * 128:(c + 1) * 128, :], [], [f"W0T_{c}"], stream="ldg",
            no_waw=True)
        dma("gpsimd", W1N[:, c * INTER:(c + 1) * INTER],
            w1n_d[c * 128:(c + 1) * 128, :], [], [f"W1N_{c}"], stream="ldg",
            no_waw=True)
    dma("gpsimd", w1n_sc.rearrange("(c p) f -> p c f", p=128),
        w1n_d.rearrange("(c p) f -> p c f", p=128), [], ["w1nsc"],
        stream="w1ns")
    vop(lambda v: v.memset(S0[:], 0.0), ["s0"], S0N)
    vop(lambda v: v.memset(S1[:], 0.0), ["s1"], S1N)
    vop(lambda v: v.memset(NORMP, 0.0), ["pan"], ["pan"])
    vop(lambda v: v.memset(SP[:], 0.0), [], ["sp"])
    vop(lambda v: v.memset(SR[:], 0.0), [], ["sr"])
    vop(lambda v: v.memset(SP[:, 0:1], 1.0), [], ["sp"])
    vop(lambda v: v.memset(SR[0:1, 1:2], 1.0), [], ["sr"])
    dma("sync", LNMF[:], lnw_d.broadcast_to((128, D)), [], ["lnmf"])
    vop(lambda v: v.memset(SLNR[:], 0.0), [], ["sln"])

    def load_w1t_chunk(ic, slot):
        dma("sync", W1TW[:, slot * 1024:(slot + 1) * 1024],
            w1n_sc[:, ic * 128:(ic + 1) * 128], ["w1nsc"], [f"w1tw{slot}"],
            stream=f"w1t{slot}", transpose=True)

    for ic in range(8):
        load_w1t_chunk(ic, ic)

    cast_jobs = []
    for e in range(NEXP):
        cast_jobs.append((W1b_d[e].rearrange("d f -> (d f)"),
                          W1f_d[e].rearrange("d f -> (d f)"), f"wc1_{e}"))
        cast_jobs.append((W2b_d[e].rearrange("f d -> (f d)"),
                          W2f_d[e].rearrange("f d -> (f d)"), f"wc2_{e}"))

    def issue_cast(i):
        if 0 <= i < len(cast_jobs):
            dst, src, nm = cast_jobs[i]
            dma("gpsimd", dst.rearrange("(a b) -> a b", a=2048),
                src.rearrange("(a b) -> a b", a=2048), [], [nm], stream="cast")

    dma("gpsimd", eb_d, zeb_d, [], ["eb"], stream="ebs")
    dma("gpsimd", ebo_d, zebo_d, [], ["ebod"], stream="ebg")
    dma("sync", KTOK[:], ktok_d[0], ["ktokd"], ["ktok"], stream="ktok")
    dma("sync", KMV[:], kmv_d[0], ["kmvd"], ["kmv"], stream="kmv")

    def scan_step(c):
        par = c % 2
        ktok = KTOK[:] if par == 0 else KTOK2
        kmv = KMV[:] if par == 0 else KMV2
        kt_n = "ktok" if par == 0 else "ktokB"
        kmv_n = "kmv" if par == 0 else "kmvB"
        ktok_nx = KTOK2 if par == 0 else KTOK[:]
        ktn_nx = "ktokB" if par == 0 else "ktok"
        for _j in range(c, len(cast_jobs), NC):
            issue_cast(_j)
        if c + 1 < NC:
            dma("sync", ktok_nx, ktok_d[c + 1], [], [ktn_nx], stream="ktok")
            dma("sync", KMV2 if par == 0 else KMV[:], kmv_d[c + 1], [],
                ["kmvB" if par == 0 else "kmv"], stream="kmv")
        vop(lambda v: v.memset(NORMP, 0.0), ["pan"], ["pan"])
        if c == 0:
            for bq in range(8):
                transp(psC[:, bq * 128:(bq + 1) * 128],
                       ktok[:, bq * 128:(bq + 1) * 128],
                       IDENTB[:], "identb", rd=[kt_n], wr=["psC"])
            vop(lambda v: v.tensor_copy(out=KT, in_=psC[:, 0:1024]), ["psC"],
                ["trash"])
        vop(lambda v: v.tensor_scalar(out=SP[:, 3:4], in0=SP[:, 0:1], scalar1=-1.0,
                                      scalar2=None, op0=ALU.mult), ["sp"], ["sp"])
        for kk in range(8):
            for n in range(4):
                mm(psA[:, n * 512:(n + 1) * 512], KT[:, kk * 128:(kk + 1) * 128],
                   W0T[:, kk * INTER + n * 512: kk * INTER + (n + 1) * 512],
                   start=(kk == 0), stop=(kk == 7), rd=["trash", f"W0T_{kk}"],
                   wr=[f"psA{n}"])
        sigtriple(SG, "bigf", psA[:], PSAF, SP[:, 3:4], UB, "bigf", DA[:],
                  "da", extra_rd=["sp"])
        vop(lambda v: v.scalar_tensor_tensor(out=HTOK[:], in0=psA[:],
                                             scalar=SP[:, 0:1], in1=SG,
                                             op0=ALU.mult, op1=ALU.mult),
            PSAF + ["bigf", "sp"], ["htok"])
        for bq in range(16):
            transp(psC[:, bq * 128:(bq + 1) * 128],
                   HTOK[:, bq * 128:(bq + 1) * 128],
                   IDENTB[:], "identb", rd=["htok"], wr=["psC"])
        act(HT[:], psC[:], AF.Copy, ["psC"], ["ht"])
        vop(lambda v: v.tensor_mul(out=UB, in0=HTOK[:], in1=SG), ["htok", "bigf"],
            ["bigf"])
        vop(lambda v: v.scalar_tensor_tensor(out=UB, in0=UB, scalar=-1.0,
                                             in1=SG, op0=ALU.mult, op1=ALU.add),
            ["bigf"], ["bigf"])
        vop(lambda v: v.tensor_tensor(out=UB, in0=UB, in1=HTOK[:], op=ALU.add),
            ["bigf", "htok"], ["bigf"])
        for ic in range(16):
            slot = ic % 8
            for n in range(2):
                mm(psB[:, n * 512:(n + 1) * 512], HT[:, ic * 128:(ic + 1) * 128],
                   W1TW[:, slot * 1024 + n * 512: slot * 1024 + (n + 1) * 512],
                   start=(ic == 0), stop=(ic == 15),
                   rd=["ht", f"w1tw{slot}"], wr=[f"psB{n}"])
            if ic < 8:
                load_w1t_chunk(ic + 8, slot)
        act(GSBf[:, 0:1024], psB[:], AF.Square, PSBF, ["gdf", "dat", "rc"],
            accum=RC[:, 4:5])
        vop(lambda v: v.tensor_scalar(out=RC[:, 6:7], in0=SP[:, 0:1],
                                      scalar1=SP[:, 0:1], scalar2=1.0 / D,
                                      op0=ALU.mult, op1=ALU.mult), ["sp"], ["rc"])
        vop(lambda v: v.tensor_scalar(out=RC[:, 5:6], in0=RC[:, 4:5],
                                      scalar1=RC[:, 6:7], scalar2=EPS,
                                      op0=ALU.mult, op1=ALU.add), ["rc"], ["rc"])
        act(RC[:, 5:6], RC[:, 5:6], AF.Ln, ["rc"], ["rc"])
        act(RC[:, 0:1], RC[:, 5:6], AF.Exp, ["rc"], ["rc"], scale=-0.5)
        vop(lambda v: v.tensor_mul(out=T0[:], in0=psB[:], in1=LNMF[:]),
            PSBF + ["lnmf"], ["t0"])
        vop(lambda v: v.tensor_scalar(out=RC[:, 6:7], in0=RC[:, 0:1],
                                      scalar1=SP[:, 0:1], scalar2=None,
                                      op0=ALU.mult), ["rc", "sp"], ["rc"])
        vop(lambda v, kmv=kmv: v.scalar_tensor_tensor(out=E0[:], in0=T0[:],
                                                      scalar=RC[:, 6:7], in1=kmv,
                                                      op0=ALU.mult, op1=ALU.add),
            ["t0", "rc", kmv_n], ["e0"])
        vop(lambda v: v.tensor_mul(out=T1[:], in0=E0[:], in1=LNMF[:]),
            ["e0", "lnmf"], ["t1"])
        vop(lambda v: v.scalar_tensor_tensor(out=GSBf[:, 0:1024], in0=psB[:],
                                             scalar=1.0,
                                             in1=T1[:], op0=ALU.mult,
                                             op1=ALU.mult, accum_out=RC[:, 1:2]),
            PSBF + ["t1"], ["gdf", "rc"])
        vop(lambda v, c=c: v.tensor_scalar(out=RC[:, 2:3], in0=THCOL[:, c:c + 1],
                                           scalar1=RC[:, 0:1], scalar2=KAPPA,
                                           op0=ALU.mult, op1=ALU.mult),
            ["pan", "rc"], ["rc"])
        vop(lambda v: v.tensor_scalar(out=RC[:, 6:7], in0=RC[:, 6:7],
                                      scalar1=RC[:, 6:7], scalar2=1.0 / D,
                                      op0=ALU.mult, op1=ALU.mult), ["rc"], ["rc"])
        vop(lambda v: v.tensor_scalar(out=RC[:, 3:4], in0=RC[:, 2:3],
                                      scalar1=RC[:, 6:7], scalar2=None,
                                      op0=ALU.mult), ["rc"], ["rc"])
        vop(lambda v: v.tensor_scalar(out=RC[:, 3:4], in0=RC[:, 3:4],
                                      scalar1=RC[:, 1:2], scalar2=None,
                                      op0=ALU.mult), ["rc"], ["rc"])
        vop(lambda v: v.tensor_scalar(out=GSBf[:, 0:1024], in0=psB[:],
                                      scalar1=RC[:, 3:4],
                                      scalar2=None, op0=ALU.mult),
            PSBF + ["rc"], ["gdf"])
        vop(lambda v: v.scalar_tensor_tensor(out=DY[:], in0=T1[:],
                                             scalar=RC[:, 2:3],
                                             in1=GSBf[:, 0:1024],
                                             op0=ALU.mult, op1=ALU.subtract),
            ["t1", "rc", "gdf"], ["dy"])
        vop(lambda v: v.tensor_scalar(out=RC[:, 6:7], in0=RC[:, 2:3],
                                      scalar1=SP[:, 0:1], scalar2=None,
                                      op0=ALU.mult), ["rc", "sp"], ["rc"])
        vop(lambda v: v.scalar_tensor_tensor(out=ZLN[:], in0=psB[:],
                                             scalar=RC[:, 6:7], in1=E0[:],
                                             op0=ALU.mult, op1=ALU.mult),
            PSBF + ["rc", "e0", "t0"], ["t0"])
        for bq in range(8):
            transp(psC[:, bq * 128:(bq + 1) * 128], DY[:, bq * 128:(bq + 1) * 128],
                   IDENTB[:], "identb", rd=["dy"], wr=["psC"])
        act(DYT[:], psC[:, 0:1024], AF.Copy, ["psC"], ["dyt", "e0"])
        for n in range(2):
            mm(psB[0:1, n * 512:(n + 1) * 512], ONESB[:],
               ZLN[:, n * 512:(n + 1) * 512], True, True, rd=["t0", "onesb"],
               wr=[f"psB{n}"])
        for kk in range(8):
            for n in range(4):
                mm(psA[:, n * 512:(n + 1) * 512], DYT[:, kk * 128:(kk + 1) * 128],
                   W1N[:, kk * INTER + n * 512: kk * INTER + (n + 1) * 512],
                   start=(kk == 0), stop=(kk == 7), rd=["dyt", f"W1N_{kk}"],
                   wr=[f"psA{n}"])
        vop(lambda v: v.scalar_tensor_tensor(out=DA[:], in0=psA[:],
                                             scalar=SP[:, 0:1], in1=UB,
                                             op0=ALU.mult, op1=ALU.mult),
            PSAF + ["bigf", "sp"], ["da"])
        for bq in range(16):
            transp(psC[:, bq * 128:(bq + 1) * 128], DA[:, bq * 128:(bq + 1) * 128],
                   IDENTB[:], "identb", rd=["da"], wr=["psC"])
        act(GDUMP, psC[:], AF.Copy, ["psC"], ["dat", "gdf"])
        for kk in range(8):
            mm(psCf[:, 0:128], DYT[:, kk * 128:(kk + 1) * 128],
               DYT[:, kk * 128:(kk + 1) * 128], start=(kk == 0), stop=(kk == 7),
               rd=["dyt", "dat"], wr=["psC"])
        for ic in range(16):
            mm(psCf[:, 128:256], HT[:, ic * 128:(ic + 1) * 128],
               HT[:, ic * 128:(ic + 1) * 128], start=(ic == 0), stop=(ic == 15),
               rd=["ht"], wr=["psC"])
        for kk in range(8):
            mm(psCf[:, 512:640], KT[:, kk * 128:(kk + 1) * 128],
               KT[:, kk * 128:(kk + 1) * 128], start=(kk == 0), stop=(kk == 7),
               rd=["trash"], wr=["psC"])
        for ic in range(16):
            mm(psCf[:, 640:768], GDUMP[:, ic * 128:(ic + 1) * 128],
               GDUMP[:, ic * 128:(ic + 1) * 128], start=(ic == 0), stop=(ic == 15),
               rd=["dat"], wr=["psC"])
        vop(lambda v: v.tensor_copy(out=BIGF[:, 0:128], in_=psCf[:, 0:128]),
            ["psC"], ["gram0"])
        vop(lambda v: v.tensor_copy(out=BIGF[:, 128:256], in_=psCf[:, 512:640]),
            ["psC"], ["gram1"])
        vop(lambda v: v.scalar_tensor_tensor(out=BIGF[:, 256:384],
                                             in0=psCf[:, 128:256], scalar=1.0,
                                             in1=BIGF[:, 0:128], op0=ALU.mult,
                                             op1=ALU.mult,
                                             accum_out=NORMP[:, 0:1]),
            ["psC", "gram0", "pan"], ["gram2", "pan"])
        vop(lambda v: v.scalar_tensor_tensor(out=BIGF[:, 384:512],
                                             in0=psCf[:, 640:768], scalar=1.0,
                                             in1=BIGF[:, 128:256], op0=ALU.mult,
                                             op1=ALU.mult,
                                             accum_out=NORMP[:, 16:17]),
            ["psC", "gram1", "pan"], ["gram2", "pan"])
        act(BIGF[0:1, 1024:2048], psB[0:1, 0:1024], AF.Square, PSBF,
            ["bigf", "pan"], accum=NORMP[0:1, 32:33])
        mm(psCf[0:1, 0:34], ONESF[:], NORMP[:, 0:34], True, True,
           rd=["pan", "onesf"], wr=["psC"])
        vop(lambda v: v.reduce_sum(out=SR[0:1, 4:5], in_=psCf[0:1, 0:34],
                                   axis=AX.X), ["psC"], ["sr"])
        vop(lambda v: v.tensor_scalar(out=SR[0:1, 4:5], in0=SR[0:1, 4:5],
                                      scalar1=1e-38, scalar2=None, op0=ALU.add),
            ["sr"], ["sr"])
        act(SR[0:1, 4:5], SR[0:1, 4:5], AF.Ln, ["sr"], ["sr"])
        act(SR[0:1, 4:5], SR[0:1, 4:5], AF.Exp, ["sr"], ["sr"], scale=0.5)
        vop(lambda v: v.tensor_scalar(out=SR[0:1, 0:1], in0=SR[0:1, 4:5],
                                      scalar1=1e-6, scalar2=None, op0=ALU.add),
            ["sr"], ["sr"])
        vop(lambda v: v.reciprocal(out=SR[0:1, 0:1], in_=SR[0:1, 0:1]), ["sr"],
            ["sr"])
        vop(lambda v: v.tensor_scalar(out=SR[0:1, 0:1], in0=SR[0:1, 0:1],
                                      scalar1=1.0, scalar2=None, op0=ALU.min),
            ["sr"], ["sr"])
        vop(lambda v: v.reciprocal(out=SR[0:1, 5:6], in_=SR[0:1, 0:1]), ["sr"],
            ["sr"])
        vop(lambda v, c=c: v.tensor_scalar(out=SR[0:1, 2:3],
                                           in0=ETACOL[0:1, c:c + 1],
                                           scalar1=SR[0:1, 1:2],
                                           scalar2=SR[0:1, 5:6],
                                           op0=ALU.mult, op1=ALU.mult),
            ["pan", "sr"], ["sr"])
        vop(lambda v, c=c: v.tensor_scalar(out=SR[0:1, 3:4],
                                           in0=BETACOL[0:1, c:c + 1],
                                           scalar1=SR[0:1, 1:2],
                                           scalar2=SR[0:1, 5:6],
                                           op0=ALU.mult, op1=ALU.mult),
            ["pan", "sr"], ["sr"])
        vop(lambda v: v.tensor_copy(out=SR[0:1, 1:2], in_=SR[0:1, 0:1]), ["sr"],
            ["sr"])
        mm(psCf[:, 0:4], ONESK1[:], SR[0:1, 0:4], True, True, rd=["sr", "onesk1"],
           wr=["psC"])
        vop(lambda v: v.tensor_copy(out=SP[:, 0:1], in_=psCf[:, 0:1]), ["psC"],
            ["sp"])
        vop(lambda v: v.tensor_copy(out=SP[:, 1:2], in_=psCf[:, 2:3]), ["psC"],
            ["sp"])
        vop(lambda v: v.tensor_copy(out=SP[:, 2:3], in_=psCf[:, 3:4]), ["psC"],
            ["sp"])
        for m in range(8):
            act(W1N[:, m * INTER:(m + 1) * INTER],
                W1N[:, m * INTER:(m + 1) * INTER], AF.Copy,
                ["sp", f"W1N_{m}"], [f"W1N_{m}"], scale=SP[:, 2:3])

        def grad_apply(lhs_src, lhs_n, rhs_ap, rhs_n, Stile, Spfx, Wtile, Wpfx,
                       pool_w, wb):
            for m in range(8):
                for s in range(4):
                    mm(psA[:, s * 512:(s + 1) * 512],
                       lhs_src[:, m * 128:(m + 1) * 128],
                       rhs_ap[:, s * 512:(s + 1) * 512],
                       True, True, rd=[lhs_n, rhs_n], wr=[f"psA{s}"])
                for h in range(2):
                    vop(lambda v, m=m, h=h: v.scalar_tensor_tensor(
                        out=Stile[:, m * INTER + h * 1024:
                                  m * INTER + h * 1024 + 1024],
                        in0=Stile[:, m * INTER + h * 1024:
                                  m * INTER + h * 1024 + 1024],
                        scalar=SP[:, 1:2],
                        in1=psA[:, h * 1024:(h + 1) * 1024],
                        op0=ALU.mult, op1=ALU.subtract),
                        [f"psA{2 * h}", f"psA{2 * h + 1}", "sp", f"{Spfx}{m}"],
                        [f"{Spfx}{m}"])
                if pool_w:
                    sch.op("gpsimd",
                           lambda g, m=m: g.tensor_tensor(
                               out=Wtile[:, m * INTER:(m + 1) * INTER],
                               in0=Wtile[:, m * INTER:(m + 1) * INTER],
                               in1=Stile[:, m * INTER:(m + 1) * INTER],
                               op=ALU.add),
                           reads=[f"{Spfx}{m}", f"{Wpfx}{m}"],
                           writes=[f"{Wpfx}{m}"])
                else:
                    vop(lambda v, m=m: v.scalar_tensor_tensor(
                        out=Wtile[:, m * INTER:(m + 1) * INTER],
                        in0=Wtile[:, m * INTER:(m + 1) * INTER],
                        scalar=SP[:, 2:3],
                        in1=Stile[:, m * INTER:(m + 1) * INTER],
                        op0=ALU.mult, op1=ALU.add),
                        ["sp", f"{Spfx}{m}", f"{Wpfx}{m}"], [f"{Wpfx}{m}"])
                if wb:
                    dma("gpsimd", w1n_sc[m * 128:(m + 1) * 128, :],
                        Wtile[:, m * INTER:(m + 1) * INTER], [f"{Wpfx}{m}"],
                        ["w1nsc"], stream="w1ns", no_waw=True)
        grad_apply(DY, "dy", HTOK[:], "htok", S1, "S1_", W1N, "W1N_",
                   True, True)
        grad_apply(ktok, kt_n, DA[:], "da", S0, "S0_", W0T, "W0T_",
                   False, False)
        if c + 1 < NC:
            for ic in range(8):
                load_w1t_chunk(ic, ic)
            for bq in range(8):
                transp(psC[:, bq * 128:(bq + 1) * 128],
                       ktok_nx[:, bq * 128:(bq + 1) * 128],
                       IDENTB[:], "identb", rd=[ktn_nx], wr=["psC"])
            vop(lambda v: v.tensor_copy(out=KT, in_=psC[:, 0:1024]), ["psC"],
                ["trash"])
        vop(lambda v: v.tensor_scalar(out=BIGF[0:1, 0:1024], in0=psB[0:1, 0:1024],
                                      scalar1=SR[0:1, 0:1], scalar2=None,
                                      op0=ALU.mult), PSBF + ["sr"], ["bigf"])
        for n in range(2):
            mm(psCf[:, n * 512:(n + 1) * 512], ONESK1[:],
               BIGF[0:1, n * 512:(n + 1) * 512], True, True,
               rd=["bigf", "onesk1"], wr=["psC"])
        vop(lambda v, c=c: v.scalar_tensor_tensor(out=SLNR[:], in0=SLNR[:],
                                                  scalar=ETACOL[:, c:c + 1],
                                                  in1=psCf[:, 0:1024],
                                                  op0=ALU.mult, op1=ALU.subtract),
            ["psC", "pan", "sln"], ["sln"])
        vop(lambda v, c=c: v.scalar_tensor_tensor(out=LNMF[:], in0=LNMF[:],
                                                  scalar=BETACOL[:, c:c + 1],
                                                  in1=SLNR[:], op0=ALU.mult,
                                                  op1=ALU.add),
            ["sln", "pan", "lnmf"], ["lnmf"])
        nm = f"state{c}"
        if nm in taps:
            tw = tapt(nm, [4, 128, 8 * INTER], BF16)
            for i, (tile, tn) in enumerate(((W0T, W0N), (W1N, W1NN),
                                            (S0, S0N), (S1, S1N))):
                dma("scalar", tw[i], tile[:], tn, [f"tap_{nm}"],
                    stream="tap", no_waw=True)
        if nm + "_sc" in taps:
            tw2 = tapt(nm + "_sc", [1, 8])
            dma("scalar", tw2, SR[0:1, :], ["sr"], [f"tap_{nm}s"],
                stream="tap", no_waw=True)
        if nm + "_ln" in taps:
            tw3 = tapt(nm + "_ln", [1, D])
            dma("scalar", tw3, LNMF[0:1, :], ["lnmf"], [f"tap_{nm}l"],
                stream="tap", no_waw=True)

    if STOP_AFTER != "A":
        for c in range(NC):
            scan_step(c if STOP_AFTER != "prep" else c)
    if STOP_AFTER in ("prep", "scan"):
        stop_here()
    sch.fence()
    W1TF = S0[:]
    for ic in range(16):
        dma("scalar", W1TF[:, ic * 1024:(ic + 1) * 1024],
            w1n_sc[:, ic * 128:(ic + 1) * 128], ["w1nsc"], ["w1tf"],
            stream="w1tC", no_waw=True, transpose=True)



    def layernorm_tile(z_ap, zn, outn, extran=()):
        ex = list(extran)
        vop(lambda v: v.reduce_sum(out=RC[:, 4:5], in_=z_ap, axis=AX.X),
            [zn] + ex, ["rc"])
        vop(lambda v: v.tensor_scalar(out=RC[:, 4:5], in0=RC[:, 4:5],
                                      scalar1=1.0 / D, scalar2=None,
                                      op0=ALU.mult), ["rc"], ["rc"])
        act(TRASH[:, 0:1024], z_ap, AF.Square, [zn] + ex, ["trash", "rc"],
            accum=RC[:, 5:6])
        vop(lambda v: v.tensor_scalar(out=RC[:, 6:7], in0=RC[:, 4:5],
                                      scalar1=RC[:, 4:5], scalar2=-1.0,
                                      op0=ALU.mult, op1=ALU.mult), ["rc"], ["rc"])
        vop(lambda v: v.scalar_tensor_tensor(out=RC[:, 5:6], in0=RC[:, 5:6],
                                             scalar=1.0 / D, in1=RC[:, 6:7],
                                             op0=ALU.mult, op1=ALU.add),
            ["rc"], ["rc"])
        vop(lambda v: v.tensor_scalar(out=RC[:, 5:6], in0=RC[:, 5:6],
                                      scalar1=LN_EPS, scalar2=1e-30, op0=ALU.add,
                                      op1=ALU.max), ["rc"], ["rc"])
        vop(lambda v: v.tensor_scalar(out=RC[:, 5:6], in0=RC[:, 5:6],
                                      scalar1=1e30, scalar2=None, op0=ALU.min),
            ["rc"], ["rc"])
        act(RC[:, 5:6], RC[:, 5:6], AF.Ln, ["rc"], ["rc"])
        act(RC[:, 5:6], RC[:, 5:6], AF.Exp, ["rc"], ["rc"], scale=-0.5)
        vop(lambda v: v.tensor_scalar(out=z_ap, in0=z_ap, scalar1=RC[:, 4:5],
                                      scalar2=None, op0=ALU.subtract),
            [zn, "rc"] + ex, [zn] + ex)
        vop(lambda v: v.tensor_scalar(out=z_ap, in0=z_ap, scalar1=RC[:, 5:6],
                                      scalar2=None, op0=ALU.mult), [zn, "rc"] + ex,
            [outn] + ex)

    def phaseC_tile(li):
        dma("sync", QTOK_C, qtok_d[li * 128:(li + 1) * 128, :],
            ["qtokd"] if li == 0 else [], ["qc"], stream="ldq")
        dma("sync", XL_C, x_loc[li * 128:(li + 1) * 128, :], [], ["xl", "bigf"], stream="ldxl")
        qn, xn = "qc", "xl"
        qt = QTOK_C
        for bq in range(8):
            transp(psC[:, bq * 128:(bq + 1) * 128], qt[:, bq * 128:(bq + 1) * 128],
                   IDENTB[:], "identb", rd=[qn], wr=["psC"])
        vop(lambda v: v.tensor_copy(out=KT, in_=psC[:, 0:1024]), ["psC"], ["trash"])
        for kk in range(8):
            for n in range(4):
                mm(psA[:, n * 512:(n + 1) * 512], KT[:, kk * 128:(kk + 1) * 128],
                   W0T[:, kk * INTER + n * 512: kk * INTER + (n + 1) * 512],
                   start=(kk == 0), stop=(kk == 7), rd=["trash"] + W0N,
                   wr=["psA"])
        vop(lambda v: v.tensor_scalar(out=SP[:, 3:4], in0=SP[:, 0:1], scalar1=-1.0,
                                      scalar2=None, op0=ALU.mult), ["sp"], ["sp"])
        sigtriple(GSBf, "gsb0", psA[:], "psA", SP[:, 3:4], GSBf, "gsb0",
                  TRASH[:], "trash", extra_rd=["sp"])
        vop(lambda v: v.scalar_tensor_tensor(out=HTOK[:], in0=psA[:],
                                             scalar=SP[:, 0:1], in1=GSBf,
                                             op0=ALU.mult, op1=ALU.mult),
            ["psA", "gsb0", "sp"], ["htok"])
        for bq in range(16):
            transp(psC[:, bq * 128:(bq + 1) * 128],
                   HTOK[:, bq * 128:(bq + 1) * 128],
                   IDENTB[:], "identb", rd=["htok"], wr=["psC"])
        vop(lambda v: v.tensor_copy(out=HT[:], in_=psC[:]), ["psC"], ["ht"])
        for ic in range(16):
            for n in range(2):
                mm(psB[:, n * 512:(n + 1) * 512], HT[:, ic * 128:(ic + 1) * 128],
                   W1TF[:, ic * 1024 + n * 512: ic * 1024 + (n + 1) * 512],
                   start=(ic == 0), stop=(ic == 15), rd=["ht", "w1tf"],
                   wr=["psB"])
        act(TRASH[:, 0:1024], psB[:], AF.Square, ["psB"], ["trash", "rc"],
            accum=RC[:, 4:5])
        vop(lambda v: v.tensor_scalar(out=RC[:, 6:7], in0=SP[:, 0:1],
                                      scalar1=SP[:, 0:1], scalar2=1.0 / D,
                                      op0=ALU.mult, op1=ALU.mult), ["sp"], ["rc"])
        vop(lambda v: v.tensor_scalar(out=RC[:, 5:6], in0=RC[:, 4:5],
                                      scalar1=RC[:, 6:7], scalar2=EPS,
                                      op0=ALU.mult, op1=ALU.add), ["rc"], ["rc"])
        act(RC[:, 5:6], RC[:, 5:6], AF.Ln, ["rc"], ["rc"])
        act(RC[:, 0:1], RC[:, 5:6], AF.Exp, ["rc"], ["rc"], scale=-0.5)
        vop(lambda v: v.tensor_mul(out=T0[:], in0=psB[:], in1=LNMF[:]),
            ["psB", "lnmf"], ["t0"])
        vop(lambda v: v.tensor_scalar(out=RC[:, 6:7], in0=RC[:, 0:1],
                                      scalar1=SP[:, 0:1], scalar2=None,
                                      op0=ALU.mult), ["rc", "sp"], ["rc"])
        vop(lambda v, qt=qt: v.scalar_tensor_tensor(out=ATT_C, in0=T0[:],
                                                    scalar=RC[:, 6:7], in1=qt,
                                                    op0=ALU.mult, op1=ALU.add),
            ["t0", "rc", qn], ["att", "bigf"])
        vop(lambda v: v.tensor_add(out=X1_CC, in0=XL_C, in1=ATT_C),
            [xn, "att", "bigf"], ["z1", "bigf"])
        layernorm_tile(X1_CC, "z1", "x1t", extran=["bigf"])
        dma("scalar", x1_sc[li * 128:(li + 1) * 128, :], X1_CC, ["x1t", "bigf"], ["x1d"], stream="x1d", no_waw=True)

    if STOP_AFTER is None or STOP_AFTER == "C":
        for li in range(NTILE_L):
            phaseC_tile(li)
    if STOP_AFTER == "C":
        stop_here()

    if tapt("x1", [TLOC, D]) is not None:
        dma("scalar", tap_out["x1"], x1_sc, ["x1d"], ["tap_x1"], stream="tap", no_waw=True)
    if tapt("finstate", [2, 128, 8 * INTER], BF16) is not None:
        dma("scalar", tap_out["finstate"][0], W0T[:], W0N, ["tap_fs"], stream="tap", no_waw=True)
        dma("scalar", tap_out["finstate"][1], W1N[:], W1NN, ["tap_fs"], stream="tap", no_waw=True)
    if tapt("finsc", [1, 8]) is not None:
        dma("scalar", tap_out["finsc"], SR[0:1, :], ["sr"], ["tap_fsc"])
    if tapt("finln", [1, D]) is not None:
        dma("scalar", tap_out["finln"], LNMF[0:1, :], ["lnmf"], ["tap_fln"])

    sch.fence()
    XT0 = W0T[:].bitcast(F32)
    XT1 = W1N[:].bitcast(F32)

    def xt_ap(t):
        return (XT0 if t < 8 else XT1)[:, (t % 8) * 1024:(t % 8 + 1) * 1024]

    def xt_nm(t):
        return f"xt{t}"

    W1St = S1[:]
    W2St = W1TW[:]

    def ffn_layer(l):
        if l == 0:
            vop(lambda v: v.memset(SCT[:, 1026:1032], 0.0), [], ["ht"])
            for t in range(NTILE_L):
                dma("sync", xt_ap(t), x1_sc[t * 128:(t + 1) * 128, :], ["x1d"],
                    [xt_nm(t)], stream="ldC", no_waw=True)
        for ch in range(8):
            dma("sync", ROUT[:, ch * 16:(ch + 1) * 16],
                routers_d[l, ch * 128:(ch + 1) * 128, :], [], ["rout"],
                stream="ldC", no_waw=True)
        vop(lambda v: v.memset(RCNT[:], 0.0), [], ["rcnt"])
        for t in range(NTILE_L):
            for h in range(2):
                for b2 in range(4):
                    bb2 = h * 4 + b2
                    transp(psCf[:, b2 * 128:(b2 + 1) * 128],
                           xt_ap(t)[:, bb2 * 128:(bb2 + 1) * 128], IDENTF[:],
                           "identf", rd=[xt_nm(t)], wr=["psC"])
                vop(lambda v, h=h: v.tensor_copy(
                    out=XTF[:, h * 512:(h + 1) * 512],
                    in_=psCf[:, 0:512]), ["psC"], ["bigf"])
            for ch in range(8):
                mm(psB[:, 0:16], XTF[:, ch * 128:(ch + 1) * 128],
                   ROUT[:, ch * 16:(ch + 1) * 16], start=(ch == 0),
                   stop=(ch == 7), rd=["bigf", "rout"], wr=["psB"])
            vop(lambda v: v.max(out=MX8[:], in_=psB[:, 0:16]), ["psB"], ["mx8"])
            vop(lambda v: v.max_index(out=IXU[:], in_max=MX8[:],
                                      in_values=psB[:, 0:16]), ["mx8", "psB"],
                ["ixu"])
            vop(lambda v: v.tensor_copy(out=IDXF[:], in_=IXU[:, 0:1]), ["ixu"],
                ["idxf"])
            vop(lambda v: v.tensor_scalar(out=RC[:, 7:8], in0=MX8[:, 0:1],
                                          scalar1=-1.0, scalar2=None,
                                          op0=ALU.mult), ["mx8"], ["rc"])
            act(EXW[:], psB[:, 0:16], AF.Exp, ["psB", "rc"], ["exw"],
                bias=RC[:, 7:8])
            vop(lambda v: v.reduce_sum(out=RC[:, 6:7], in_=EXW[:], axis=AX.X),
                ["exw"], ["rc"])
            vop(lambda v: v.reciprocal(out=RC[:, 5:6], in_=RC[:, 6:7]), ["rc"],
                ["rc"])
            vop(lambda v: v.tensor_tensor(out=ISF[:], in0=IOTA[:],
                                          in1=IDXF[:].broadcast_to((128, 16)),
                                          op=ALU.is_equal), ["idxf", "iota"],
                ["isf"])
            vop(lambda v: v.tensor_copy(out=ISB[:], in_=ISF[:]), ["isf"], ["isb"])
            mm(psB[:, 16:32], TRIB[:], ISB[:], True, True, rd=["isb", "trib"],
               wr=["psB"])
            mm(psB[0:1, 32:48], ONESB[:], ISB[:], True, True, rd=["isb", "onesb"],
               wr=["psB"])
            mm(psB[:, 48:64], ONESK1[:], RCNT[:], True, True,
               rd=["rcnt", "onesk1"], wr=["psB"])
            vop(lambda v: v.tensor_copy(out=POSW[:], in_=psB[:, 16:32]), ["psB"],
                ["posw"])
            vop(lambda v: v.tensor_tensor(out=POSW[:], in0=POSW[:],
                                          in1=psB[:, 48:64], op=ALU.add),
                ["posw", "psB"], ["posw"])
            vop(lambda v: v.tensor_tensor(out=RCNT[:], in0=RCNT[:],
                                          in1=psB[0:1, 32:48], op=ALU.add),
                ["rcnt", "psB"], ["rcnt"])
            vop(lambda v: v.tensor_mul(out=ISF[:], in0=POSW[:], in1=ISF[:]),
                ["posw", "isf"], ["isf"])
            vop(lambda v: v.reduce_sum(out=SLOTF[:], in_=ISF[:], axis=AX.X),
                ["isf"], ["slotf"])
            vop(lambda v: v.tensor_scalar(out=RC[:, 7:8], in0=SLOTF[:],
                                          scalar1=float(CAP), scalar2=None,
                                          op0=ALU.is_ge), ["slotf"], ["rc"])
            vop(lambda v: v.scalar_tensor_tensor(out=SLOTF[:], in0=IDXF[:],
                                                 scalar=float(CAP), in1=SLOTF[:],
                                                 op0=ALU.mult, op1=ALU.add),
                ["idxf", "slotf"], ["slotf"])
            vop(lambda v: v.tensor_scalar(out=RC[:, 6:7], in0=RC[:, 7:8],
                                          scalar1=-1.0, scalar2=1.0,
                                          op0=ALU.mult, op1=ALU.add), ["rc"],
                ["rc"])
            vop(lambda v: v.tensor_mul(out=SLOTF[:], in0=SLOTF[:],
                                       in1=RC[:, 6:7]), ["slotf", "rc"],
                ["slotf"])
            vop(lambda v: v.scalar_tensor_tensor(out=SLOTF[:], in0=RC[:, 7:8],
                                                 scalar=float(DUMP),
                                                 in1=SLOTF[:], op0=ALU.mult,
                                                 op1=ALU.add),
                ["rc", "slotf"], ["slotf"])
            vop(lambda v, t=t: v.tensor_copy(out=SLOT_ALL[:, t:t + 1],
                                             in_=SLOTF[:]), ["slotf"],
                ["slotall"])
            vop(lambda v, t=t: v.tensor_copy(out=SCT[:, 0:1024], in_=xt_ap(t)),
                [xt_nm(t)], ["ht"])
            vop(lambda v: v.tensor_copy(out=SCT[:, 1024:1025],
                                        in_=RC[:, 5:6]), ["rc"], ["ht"])
            vop(lambda v: v.tensor_tensor(out=SCT[:, 1025:1026],
                                          in0=RC[:, 5:6],
                                          in1=SCT[:, 1024:1025],
                                          op=ALU.subtract), ["rc", "ht"], ["ht"])
            vop(lambda v: v.tensor_copy(out=SLOTI[:, 0:1], in_=SLOTF[:]),
                ["slotf"], ["sloti"])
            sch.op("gpsimd",
                   lambda g: g.indirect_dma_start(
                       out=eb_d, out_offset=IndirectOffsetOnAxis(SLOTI[:, 0:1], 0),
                       in_=SCT, in_offset=None),
                   reads=["ht", "sloti"], writes=["eb"], inc=16, stream="ebs",
                   no_waw=True)
        for e in range(NEXP):
            for st in range(2):
                dma("sync", GBv,
                    eb_d[e * CAP + st * 128: e * CAP + (st + 1) * 128, :],
                    ["eb", "bigf"], ["bigf"], stream="gbg")
                vop(lambda v, st=st: v.tensor_copy(
                    out=SELG[:, st:st + 1],
                    in_=GBv[:, 1024:1025]), ["bigf"], ["selg"])
                vop(lambda v, st=st: v.tensor_tensor(
                    out=SELG[:, st:st + 1], in0=SELG[:, st:st + 1],
                    in1=GBv[:, 1025:1026], op=ALU.add), ["bigf", "selg"],
                    ["selg"])
                for bq in range(8):
                    transp(psC[:, bq * 128:(bq + 1) * 128],
                           GBv[:, bq * 128:(bq + 1) * 128],
                           IDENTB[:], "identb", rd=["bigf"], wr=["psC"])
                vop(lambda v, st=st: v.tensor_copy(
                    out=XTGv[:, st * 1024:(st + 1) * 1024], in_=psC[:, 0:1024]),
                    ["psC"], ["xtg", "bigf"])
            for hh in range(2):
                for half, qeng, snm in ((0, "sync", "w1stA"),
                                        (1, "gpsimd", "w1stB")):
                    dma(qeng,
                        W1St[:, half * 8192: half * 8192 + 8192]
                        .rearrange("p (c f) -> p c f", c=4),
                        W1b_d[e, half * 512:(half + 1) * 512,
                              hh * 2048:(hh + 1) * 2048]
                        .rearrange("(c p) f -> p c f", p=128),
                        [snm], [snm], stream=snm,
                        waits=sch.barrier_of([f"wc1_{e}"]))
                for st in range(2):
                    for kk in range(8):
                        for n in range(4):
                            mm(psA[:, n * 512:(n + 1) * 512],
                               XTGv[:, st * 1024 + kk * 128:
                                    st * 1024 + (kk + 1) * 128],
                               W1St[:, kk * 2048 + n * 512:
                                    kk * 2048 + (n + 1) * 512],
                               start=(kk == 0), stop=(kk == 7),
                               rd=["xtg", "w1stA" if kk < 4 else "w1stB"],
                               wr=["psA"])
                    sigtriple(DA[:], "da", psA[:], "psA", -1.0, GSBf, "gsb0",
                              TRASH[:], "trash")
                    vop(lambda v, st=st, hh=hh: v.scalar_tensor_tensor(
                        out=HFv[:, st * FF + hh * 2048:
                                st * FF + (hh + 1) * 2048],
                        in0=psA[:], scalar=1.0, in1=DA[:], op0=ALU.mult,
                        op1=ALU.mult), ["psA", "da"], [f"hf{st}", "S0"])
            for st in range(2):
                for g in range(2):
                    for bq in range(16):
                        transp(psC[:, bq * 128:(bq + 1) * 128],
                               HFv[:, st * FF + g * 2048 + bq * 128:
                                   st * FF + g * 2048 + (bq + 1) * 128],
                               IDENTB[:], "identb", rd=[f"hf{st}"], wr=["psC"])
                    vop(lambda v, st=st, g=g: v.tensor_copy(
                        out=HFv[:, st * FF + g * 2048: st * FF + (g + 1) * 2048],
                        in_=psC[:]), ["psC"], [f"hf{st}", "S0"])
            psY = {0: psB[:, 0:1024], 1: psA[:, 0:1024]}
            psYn = {0: "psB", 1: "psA"}
            for q in range(4):
                for half, qeng, snm in ((0, "sync", "w2sA"),
                                        (1, "gpsimd", "w2sB")):
                    dma(qeng,
                        W2St[:, half * 4096: half * 4096 + 4096]
                        .rearrange("p (c f) -> p c f", c=4),
                        W2b_d[e, q * 1024 + half * 512:
                              q * 1024 + (half + 1) * 512, :]
                        .rearrange("(c p) f -> p c f", p=128),
                        (["w1tw%d" % i for i in range(8)]
                         if (l == 0 and e == 0 and q == 0) else [snm]),
                        [snm], stream=snm,
                        waits=sch.barrier_of([f"wc2_{e}"]))
                for st in range(2):
                    for kk in range(8):
                        ic = q * 8 + kk
                        for n in range(2):
                            mm(psY[st][:, n * 512:(n + 1) * 512],
                               HFv[:, st * FF + ic * 128:
                                   st * FF + (ic + 1) * 128],
                               W2St[:, kk * 1024 + n * 512:
                                    kk * 1024 + (n + 1) * 512],
                               start=(ic == 0), stop=(ic == 31),
                               rd=[f"hf{st}", "w2sA" if kk < 4 else "w2sB"],
                               wr=[psYn[st]])
            for st in range(2):
                act(OGv[:, st * 1024:(st + 1) * 1024], psY[st], AF.Copy,
                    [psYn[st], "selg"], [f"og{st}", "S0"],
                    scale=SELG[:, st:st + 1])
                dma("scalar",
                    ebo_d[e * CAP + st * 128: e * CAP + (st + 1) * 128, :],
                    OGv[:, st * 1024:(st + 1) * 1024], [f"og{st}"], ["ebod"],
                    stream="ebod", no_waw=True)
        for t in range(NTILE_L):
            vop(lambda v, t=t: v.tensor_copy(out=SLOTI[:, 1:2],
                                             in_=SLOT_ALL[:, t:t + 1]),
                ["slotall"], ["sloti2"])
            if l < NLAY - 1:
                sch.op("gpsimd",
                       lambda g, t=t: g.indirect_dma_start(
                           out=xt_ap(t), out_offset=None,
                           in_=ebo_d,
                           in_offset=IndirectOffsetOnAxis(SLOTI[:, 1:2], 0)),
                       reads=["sloti2", "ebod"], writes=[xt_nm(t)], inc=16,
                       stream="ebg", no_waw=True)
            else:
                sch.op("gpsimd",
                       lambda g: g.indirect_dma_start(
                           out=OGv[:, 0:1024], out_offset=None,
                           in_=ebo_d,
                           in_offset=IndirectOffsetOnAxis(SLOTI[:, 1:2], 0)),
                       reads=["sloti2", "ebod"], writes=["og0", "S0"], inc=16,
                       stream="ebg")
                dma("sync", X1T_F, x1_sc[t * 128:(t + 1) * 128, :], ["x1d"],
                    ["x1tf", "S0"], stream="ldxl")
                vop(lambda v: v.tensor_add(out=X1_CF, in0=X1T_F,
                                           in1=OGv[:, 0:1024]),
                    ["x1tf", "og0"], ["z1", "S0"])
                layernorm_tile(X1_CF, "z1", "x2t", extran=["S0"])
                dma("sync", out_x2[t * 128:(t + 1) * 128, :], X1_CF,
                    ["x2t", "S0"], ["outd"], stream="outd", no_waw=True)
        if f"ffn{l}" in taps and l < NLAY - 1:
            tw = tapt(f"ffn{l}", [TLOC, D])
            for t in range(NTILE_L):
                dma("scalar", tw[t * 128:(t + 1) * 128, :], xt_ap(t),
                    [xt_nm(t)], [f"tap_ffn{l}"], stream="tap", no_waw=True)

    if STOP_AFTER is None:
        for l in range(min(NLAY, FFNPART)):
            ffn_layer(l)

    if STOP_AFTER is None:
        if FFNPART < NLAY:
            stop_here()
        else:
            tapnames = [nm for nm in sch.bufs if nm.startswith("tap_")]
            sch.final_waits = sch.barrier_of(["outd"] + tapnames)

    with ExitStack() as semstack:
        sems = {k: semstack.enter_context(nc.semaphore("sem_" + str(i)))
                for i, k in enumerate(sch.sem_keys)}
        with nc.Block() as block:
            def mkbody(en):
                def run(eng):
                    for (waits, fn, key, inc) in sch.ops[en]:
                        for pk, pv in waits.items():
                            eng.wait_ge(sems[pk], pv)
                        inst = fn(eng)
                        inst.then_inc(sems[key], inc)
                    if en == "sync":
                        for pk, pv in sch.final_waits:
                            eng.wait_ge(sems[pk], pv)
                return run
            block.tensor(mkbody("tensor"))
            block.vector(mkbody("vector"))
            block.scalar(mkbody("scalar"))
            block.gpsimd(mkbody("gpsimd"))
            block.sync(mkbody("sync"))
        stack.close()
    return nc, tap_out


_cache = {}


def _get_program(taps=()):
    key = tuple(sorted(taps))
    if key not in _cache:
        _cache[key] = build_program(taps)
    return _cache[key]


def _consts():
    c = {}
    c["identb"] = np.eye(128, dtype=ml_dtypes.bfloat16)
    c["identf"] = np.eye(128, dtype=np.float32)
    c["trib"] = np.triu(np.ones((128, 128)), 1).astype(ml_dtypes.bfloat16)
    c["iota16"] = np.arange(16, dtype=np.float32).reshape(1, 16)
    c["ones128b"] = np.ones((128, 1), dtype=ml_dtypes.bfloat16)
    c["onesK1f"] = np.ones((1, 128), dtype=np.float32)
    h = np.zeros((128, 2), dtype=np.float32)
    h[0:64, 0] = 1.0
    h[64:128, 1] = 1.0
    c["halfones"] = h
    c["zeb"] = np.zeros((EBROWS, 1032), dtype=ml_dtypes.bfloat16)
    c["zebo"] = np.zeros((EBROWS, 1024), dtype=np.float32)
    return c


def make_inmaps(inputs):
    x = np.ascontiguousarray(np.asarray(inputs["L0_attn_x"], dtype=np.float32)
                             .reshape(TOK, D))
    base = dict(
        x_full=x,
        wkT=np.ascontiguousarray(np.asarray(inputs["L0_attn_wk_w"]).T),
        wvT=np.ascontiguousarray(np.asarray(inputs["L0_attn_wv_w"]).T),
        wqT=np.ascontiguousarray(np.asarray(inputs["L0_attn_wq_w"]).T),
        knorm=np.ascontiguousarray(np.asarray(inputs["L0_attn_k_norm_w"])
                                   .reshape(1, D)),
        qnorm=np.ascontiguousarray(np.asarray(inputs["L0_attn_q_norm_w"])
                                   .reshape(1, D)),
        threp=np.ascontiguousarray(np.tile(
            np.asarray(inputs["L0_attn_theta_w"]).reshape(CHUNK, D), (2, 1))),
        alrep=np.ascontiguousarray(np.tile(
            np.asarray(inputs["L0_attn_alpha_w"]).reshape(CHUNK, D), (2, 1))),
        etrep=np.ascontiguousarray(np.tile(
            np.asarray(inputs["L0_attn_eta_w"]).reshape(CHUNK, D), (2, 1))),
        w0T=np.ascontiguousarray(np.asarray(inputs["L0_attn_mem_w0"]).T),
        w1n=np.ascontiguousarray(np.asarray(inputs["L0_attn_mem_w1"])),
        lnw=np.ascontiguousarray(np.asarray(inputs["L0_attn_mem_ln_w"])
                                 .reshape(1, D)),
        routers=np.ascontiguousarray(np.asarray(inputs["L0_ffn_routers"])),
        W1f=np.ascontiguousarray(np.asarray(inputs["L0_ffn_W1"])),
        W2f=np.ascontiguousarray(np.asarray(inputs["L0_ffn_W2"])),
        **_consts(),
    )
    in_maps = []
    for ci in range(NCORE):
        m = dict(base)
        m["x_loc"] = np.ascontiguousarray(x[ci * TLOC:(ci + 1) * TLOC])
        in_maps.append(m)
    return in_maps


def kernel(**inputs):
    nc, _ = _get_program()
    in_maps = make_inmaps(inputs)
    res = run_bass_kernel_spmd(nc, in_maps, core_ids=list(range(NCORE)))
    out = np.concatenate([np.asarray(r["out_x2"]) for r in res.results], axis=0)
    return out.reshape(B, S, D).astype(np.float32)
